# version 75
# baseline (speedup 1.0000x reference)
import sys
sys.path.insert(0, "/opt/trn_rl_repo")
from contextlib import ExitStack

import sys
import numpy as np
import concourse.bass as bass
import concourse.mybir as mybir
from concourse.bass_utils import run_bass_kernel_spmd

F32 = mybir.dt.float32
BF16 = mybir.dt.bfloat16
AF = mybir.ActivationFunctionType
ALU = mybir.AluOpType
AX = mybir.AxisListType

S = 8192
C = 1024
H = 16
HD = 64
T = 256
NT = S // T
NS = T // 128
NCH = C // 128
LN_EPS = 1e-5
AT_EPS = 1e-6

ENGINES = ["sync", "scalar", "vector", "tensor", "gpsimd"]


class Em:

    OWN = {"vector": "dve", "scalar": "act", "gpsimd": "pool"}

    def __init__(self, block, sems):
        self.block = block
        self.sems = sems
        self.log = {e: [] for e in ENGINES}
        self.cnt = {s: 0 for s in sems}
        self.lastw = {}
        self.last_evt = {}
        self.pend = {e: [] for e in ENGINES}

    def wait(self, eng, evt):
        if evt is None:
            return
        if isinstance(evt, list):
            for e2 in evt:
                self.wait(eng, e2)
            return
        sem, v = evt
        k = (eng, sem)
        if self.lastw.get(k, 0) >= v:
            return
        self.lastw[k] = v
        h = self.sems[sem]
        self.pend[eng].append(lambda E, h=h, v=v: E.wait_ge(h, v))

    def waits(self, eng, evts):
        for e in evts:
            self.wait(eng, e)

    def raw(self, eng, fn):
        self.pend[eng].append(fn)

    def op(self, eng, fn, sem=None, step=1):
        self.log[eng].append((sys._getframe(1).f_lineno,
                              "dma" if (sem is not None and step == 16) else "op"))
        own = self.OWN.get(eng)
        if own is not None and not (sem is not None and step == 16):
            prev = self.cnt[own]
            if prev > self.lastw.get((eng, own), 0):
                self.lastw[(eng, own)] = prev
                hw = self.sems[own]
                self.pend[eng].append(lambda E, h=hw, v=prev: E.wait_ge(h, v))
            self.cnt[own] += 1
            v = self.cnt[own]
            h = self.sems[own]
            self.pend[eng].append(lambda E, fn=fn, h=h: fn(E).then_inc(h, 1))
            self.last_evt[eng] = (own, v)
            return (own, v)
        if sem is None:
            self.pend[eng].append(fn)
            return None
        self.cnt[sem] += step
        v = self.cnt[sem]
        h = self.sems[sem]
        self.pend[eng].append(lambda E, fn=fn, h=h, step=step: fn(E).then_inc(h, step))
        return (sem, v)

    def close(self, eng, sem, step=1):
        if eng in self.OWN:
            return self.last_evt[eng]
        fn = self.pend[eng].pop()
        self.cnt[sem] += step
        v = self.cnt[sem]
        h = self.sems[sem]
        self.pend[eng].append(lambda E, fn=fn, h=h, step=step: fn(E).then_inc(h, step))
        return (sem, v)

    def flush(self):
        for eng in ENGINES:
            ops = self.pend[eng]
            if not ops:
                continue
            self.pend[eng] = []

            def body(E, ops=ops):
                for f in ops:
                    f(E)
            getattr(self.block, eng)(body)


class BankPool:
    def __init__(self, em, pp):
        self.em = em
        self.pp = pp
        self.free_evt = [None] * 8
        self.ptr = 0
        self.sptr = 0
        self.qptr = 0
        self.fptr = 0
        self.excluded = set()
        self.small = ()
        self.qpair = ()

    def _take(self, b):
        if self.free_evt[b]:
            for evt in self.free_evt[b]:
                self.em.wait("tensor", evt)
        self.free_evt[b] = None
        return b

    def acq(self):
        for _ in range(16):
            b = self.ptr % 8
            self.ptr += 1
            if b in self.excluded or b in self.small:
                continue
            return self._take(b)
        raise RuntimeError("no psum bank available")

    def acq_small(self):
        if not self.small:
            return self.acq()
        b = self.small[self.sptr % len(self.small)]
        self.sptr += 1
        return self._take(b)

    def acq_q(self):
        if not self.qpair:
            return self.acq()
        b = self.qpair[self.qptr % len(self.qpair)]
        self.qptr += 1
        return self._take(b)

    def acq_ffn(self):
        pool = (0, 1, 2, 3, 6, 7)
        b = pool[self.fptr % len(pool)]
        self.fptr += 1
        return self._take(b)

    def rel(self, b, evts):
        self.free_evt[b] = [e for e in evts if e is not None]


def build_program(nta=NT, ntb=NT, dbg=False):
    nc = bass.Bass("TRN2", target_bir_lowering=False)

    xT = nc.dram_tensor("xT", [C, S], F32, kind="ExternalInput")
    outT = nc.dram_tensor("outT", [C, S], F32, kind="ExternalOutput")
    WNAMES = ["wq", "wk", "wv", "wo", "wr", "wk2", "wv2"]
    wd = {n: nc.dram_tensor(n, [C, C], F32, kind="ExternalInput") for n in WNAMES}
    bqkv_d = nc.dram_tensor("bqkv", [1, 3 * C], F32, kind="ExternalInput")
    bo8_d = nc.dram_tensor("bo8", [128, 8], F32, kind="ExternalInput")
    mur_d = nc.dram_tensor("mur8", [128, 8], F32, kind="ExternalInput")
    muk_d = nc.dram_tensor("muk8", [128, 8], F32, kind="ExternalInput")
    id_d = nc.dram_tensor("id128", [128, 128], F32, kind="ExternalInput")

    def x_tile_ap(dram, i):
        return bass.AP(dram, T * i, [[S, 128], [128 * S, 8], [1, T]])

    def w_ap(dram):
        return bass.AP(dram, 0, [[C, 128], [128 * C, 8], [1, C]])

    with ExitStack() as st:
        e = st.enter_context
        w_sb = {}
        for n in ["wq", "wo", "wr", "wk2", "wv2"]:
            w_sb[n] = e(nc.sbuf_tensor("w_" + n, [128, 8 * C], BF16))

        bqkv = e(nc.sbuf_tensor("sb_bqkv", [1, 3 * C], BF16))
        bo8 = e(nc.sbuf_tensor("sb_bo8", [128, 8], F32))
        mur = e(nc.sbuf_tensor("mur", [128, 8], F32))
        muk = e(nc.sbuf_tensor("muk", [128, 8], F32))
        murc = e(nc.sbuf_tensor("murc", [128, 8], F32))
        mukc = e(nc.sbuf_tensor("mukc", [128, 8], F32))
        ones_rb = e(nc.sbuf_tensor("ones_rb", [1, 256], BF16))
        zrow = e(nc.sbuf_tensor("zrow", [1, 128], BF16))
        ones_cb = e(nc.sbuf_tensor("ones_cb", [128, 1], BF16))
        ones_rf = e(nc.sbuf_tensor("ones_rf", [1, 128], F32))
        nones_rf = e(nc.sbuf_tensor("nones_rf", [1, 128], F32))
        ksumB = e(nc.sbuf_tensor("ksumB", [128, C], BF16))
        kv_sb = e(nc.sbuf_tensor("kv_sb", [128, 512], BF16))
        stats = e(nc.sbuf_tensor("stats", [1, 1792], F32))
        sm16 = e(nc.sbuf_tensor("sm16", [128, 80], F32))
        carry = e(nc.sbuf_tensor("carry", [128, 16], F32))
        cst = e(nc.sbuf_tensor("cst", [1, 4], F32))
        id_sb = e(nc.sbuf_tensor("id_sb", [128, 128], BF16))

        xbf = [e(nc.sbuf_tensor("xbf0", [128, 2048], BF16))]
        tmpf = e(nc.sbuf_tensor("tmpf", [128, 1024], F32))
        kq = e(nc.sbuf_tensor("kq", [128, 2048], BF16))
        qsb = e(nc.sbuf_tensor("qsb", [128, 2048], BF16))

        kfb_cm = nc.sbuf_tensor("kfb", [1, C], BF16)
        ksacc_cm = nc.sbuf_tensor("ksacc", [1, 2 * C], F32)
        wk_cm = nc.sbuf_tensor("w_wk", [128, 8 * C], BF16)
        wv_cm = nc.sbuf_tensor("w_wv", [128, 8 * C], BF16)
        xbfa_cm = nc.sbuf_tensor("xbf_a", [128, 2048], BF16)
        kfb = kfb_cm.__enter__()
        ksacc = ksacc_cm.__enter__()
        w_sb["wk"] = wk_cm.__enter__()
        w_sb["wv"] = wv_cm.__enter__()
        xbf_a = xbfa_cm.__enter__()

        pp = [e(nc.psum_tensor(f"pp{k}", [128, 512], F32)) for k in range(8)]

        semnames = ["pe", "dve", "act", "pool", "dxb0", "dxb1", "dxf0",
                    "dxf1", "dtp", "dst", "dst2", "dwA", "dwB", "dms", "ddbg",
                    "dwk0", "dwk1", "dwk2", "dwk3", "dwv0", "dwv1"]
        sems = {s: e(nc.semaphore("sem_" + s)) for s in semnames}

        block = e(nc.Block())
        em = Em(block, sems)
        BK = BankPool(em, pp)

        em.op("gpsimd", lambda E: E.memset(ones_rb[:], 1.0))
        em.op("gpsimd", lambda E: E.memset(zrow[:], 0.0))
        em.op("gpsimd", lambda E: E.memset(ones_cb[:], 1.0))
        em.op("gpsimd", lambda E: E.memset(ones_rf[:], 1.0))
        em.op("gpsimd", lambda E: E.memset(nones_rf[:], -1.0))
        em.op("gpsimd", lambda E: E.memset(carry[:], 0.0))
        em.op("gpsimd", lambda E: E.memset(cst[0:1, 0:1], LN_EPS))
        ev_init = em.op("gpsimd", lambda E: E.memset(ksacc[:], 0.0))
        ev_ones = ev_init

        ev_w = {}
        ev_wk = []
        for m in range(4):
            ap = bass.AP(wd["wk"], 128 * C * 2 * m, [[C, 128], [128 * C, 2], [1, C]])
            ev_wk.append(em.op("gpsimd", lambda E, m=m, ap=ap: E.dma_start(
                w_sb["wk"][:, 2 * C * m: 2 * C * (m + 1)], ap), "dwk%d" % m, 16))
        ev_wv = []
        for m in range(2):
            ap = bass.AP(wd["wv"], 128 * C * 4 * m, [[C, 128], [128 * C, 4], [1, C]])
            ev_wv.append(em.op("gpsimd", lambda E, m=m, ap=ap: E.dma_start(
                w_sb["wv"][:, 4 * C * m: 4 * C * (m + 1)], ap), "dwv%d" % m, 16))
        ev_b = em.op("gpsimd", lambda E: E.dma_start(bqkv[:], bqkv_d.ap()), "dwA", 16)
        ev_w["wk"] = ev_wk[-1]
        ev_w["wv"] = ev_wv[-1]
        ev_id = em.op("gpsimd", lambda E: E.dma_start(id_sb[:], id_d.ap()), "ddbg", 16)
        bw_names = ["wq", "wo", "wr", "wk2", "wv2"]

        def load_B_weight(n):
            em.op("gpsimd", lambda E, n=n: E.dma_start(w_sb[n][:], w_ap(wd[n])),
                  "dwB", 16)
        for n in ["wq", "wo", "wr", "wk2", "wv2"]:
            ev_w[n] = ("dwB", 80)
        em.op("sync", lambda E: E.dma_start(bo8[:], bo8_d.ap()), "dms", 16)
        em.op("sync", lambda E: E.dma_start(mur[:], mur_d.ap()), "dms", 16)
        ld_ms = em.op("sync", lambda E: E.dma_start(muk[:], muk_d.ap()), "dms", 16)
        em.waits("vector", [ld_ms])
        em.op("vector", lambda E: E.tensor_scalar(murc[:], mur[:], -1.0, 1.0,
                                                  ALU.mult, ALU.add))
        ev_muc = em.op("vector", lambda E: E.tensor_scalar(mukc[:], muk[:], -1.0, 1.0,
                                                           ALU.mult, ALU.add))
        em.flush()

        BK.excluded = {6}
        PKV = pp[6]
        em.waits("tensor", [ev_ones, ev_b])
        em.op("tensor", lambda E: E.matmul(
            PKV[:, 0:512], zrow[0:1, 0:128], bqkv[0:1, 0:512],
            start=True, stop=False, skip_group_check=True))
        ks_cur = 0
        ev_xbf_free = {0: None, 1: None}
        ev_kv_last = None
        ev_ks_last = None
        st_a = {"xbfev": {}, "kqfree": {0: None, 1: None},
                "vqfree": {0: None, 1: None}, "kkern": {}, "vout": {},
                "tmpf_free": None, "ksl": None}
        xbufs = [xbf[0], xbf_a]

        def a_pref(i):
            if i >= nta:
                return
            em.wait("gpsimd", ev_xbf_free[i % 2])
            st_a["xbfev"][i % 2] = em.op(
                "gpsimd", lambda E, i=i, b=xbufs[i % 2]: E.dma_start(
                    b[:], x_tile_ap(xT, i)), "dxb%d" % (i % 2), 16)

        def a_proj(gs):
            i, s = divmod(gs, 2)
            sl = gs % 2
            xb = xbufs[i % 2]
            ev_xbf = st_a["xbfev"][i % 2]
            em.waits("tensor", [ev_xbf, ev_b, ev_ones])
            kb = [BK.acq(), BK.acq()]
            for hh in range(2):
                em.op("tensor", lambda E, b=kb[hh], hh=hh: E.matmul(
                    pp[b][:], ones_rb[0:1, 0:128],
                    bqkv[0:1, C + 512 * hh: C + 512 * hh + 512],
                    start=True, stop=False, skip_group_check=True))
            for j in range(NCH):
                em.wait("tensor", ev_wk[j // 2])
                for hh in range(2):
                    em.op("tensor", lambda E, b=kb[hh], hh=hh, j=j, s=s, xb=xb: E.matmul(
                        pp[b][:],
                        xb[:, 256 * j + 128 * s: 256 * j + 128 * s + 128],
                        w_sb["wk"][:, C * j + 512 * hh: C * j + 512 * hh + 512],
                        start=False, stop=(j == NCH - 1), skip_group_check=True))
            ev_kp = em.close("tensor", "pe")

            em.wait("vector", ev_kp)
            for hh in range(2):
                em.op("vector", lambda E, b=kb[hh], hh=hh: E.reduce_max(
                    sm16[:, 8 * hh: 8 * hh + 8],
                    pp[b][:].rearrange("p (h d) -> p h d", h=8), axis=AX.X))
            ev_kmax = em.close("vector", "dve")

            em.waits("scalar", [ev_kp, st_a["tmpf_free"], ev_kmax])
            for hh in range(2):
                em.op("scalar", lambda E, b=kb[hh], hh=hh: E.activation(
                    tmpf[:, 512 * hh: 512 * hh + 512], pp[b][:], AF.Exp))
            ev_expk = em.close("scalar", "act")
            em.wait("scalar", ev_kmax)
            ev_em = em.op("scalar", lambda E: E.activation(
                sm16[:, 16:32], sm16[:, 0:16], AF.Exp, scale=-1.0))
            BK.rel(kb[0], [ev_kmax, ev_expk])
            BK.rel(kb[1], [ev_kmax, ev_expk])

            em.waits("vector", [ev_expk, ev_em, st_a["kqfree"][sl]])
            emB = bass.AP(sm16, 16, [[80, 128], [1, H], [0, HD]])
            ev_kkern = em.op("vector", lambda E, emB=emB, sl=sl: E.tensor_tensor(
                kq[:, 1024 * sl: 1024 * sl + 1024].rearrange("p (h d) -> p h d", h=H),
                tmpf[:].rearrange("p (h d) -> p h d", h=H),
                emB, ALU.mult))
            st_a["kkern"][gs] = ev_kkern
            st_a["tmpf_free"] = ev_kkern

            vb = [BK.acq(), BK.acq()]
            for hh in range(2):
                em.op("tensor", lambda E, b=vb[hh], hh=hh: E.matmul(
                    pp[b][:], ones_rb[0:1, 0:128],
                    bqkv[0:1, 2 * C + 512 * hh: 2 * C + 512 * hh + 512],
                    start=True, stop=False, skip_group_check=True))
            for j in range(NCH):
                em.wait("tensor", ev_wv[j // 4])
                for hh in range(2):
                    em.op("tensor", lambda E, b=vb[hh], hh=hh, j=j, s=s, xb=xb: E.matmul(
                        pp[b][:],
                        xb[:, 256 * j + 128 * s: 256 * j + 128 * s + 128],
                        w_sb["wv"][:, C * j + 512 * hh: C * j + 512 * hh + 512],
                        start=False, stop=(j == NCH - 1), skip_group_check=True))
            ev_vp = em.close("tensor", "pe")
            if s == NS - 1:
                ev_xbf_free[i % 2] = ev_vp
                a_pref(i + 2)
            em.waits("vector", [ev_vp, st_a["vqfree"][sl]])
            for hh in range(2):
                em.op("vector", lambda E, b=vb[hh], hh=hh, sl=sl: E.tensor_copy(
                    qsb[:, 1024 * sl + 512 * hh: 1024 * sl + 512 * hh + 512],
                    pp[b][:]))
            ev_vout = em.close("vector", "dve")
            st_a["vout"][gs] = ev_vout
            BK.rel(vb[0], [ev_vout])
            BK.rel(vb[1], [ev_vout])

        def a_kv(gs):
            nonlocal ks_cur, ev_kv_last, ev_ks_last
            sl = gs % 2
            em.waits("tensor", [st_a["kkern"].pop(gs), st_a["vout"].pop(gs)])
            for h in range(H):
                r, jj = h % 2, h // 2
                em.op("tensor", lambda E, r=r, jj=jj, h=h, sl=sl: E.matmul(
                    PKV[64 * r: 64 * r + 64, 64 * jj: 64 * jj + 64],
                    kq[:, 1024 * sl + HD * h: 1024 * sl + HD * h + HD],
                    qsb[:, 1024 * sl + HD * h: 1024 * sl + HD * h + HD],
                    start=False, stop=False, skip_group_check=True))
            ev_kv = em.close("tensor", "pe")
            ev_kv_last = ev_kv
            st_a["vqfree"][sl] = ev_kv

            sb_ = [BK.acq(), BK.acq()]
            for hh in range(2):
                em.op("tensor", lambda E, b=sb_[hh], hh=hh, sl=sl: E.matmul(
                    pp[b][0:1, 0:512], ones_cb[:, 0:1],
                    kq[:, 1024 * sl + 512 * hh: 1024 * sl + 512 * hh + 512],
                    start=True, stop=True, skip_group_check=True))
            ev_ksmm = em.close("tensor", "pe")
            st_a["kqfree"][sl] = ev_ksmm
            nxt = 1 - ks_cur
            em.waits("vector", [ev_ksmm, ev_init, ev_ks_last])
            for hh in range(2):
                em.op("vector", lambda E, hh=hh, cur=ks_cur, nxt=nxt, b=sb_[hh]:
                      E.tensor_tensor(
                          ksacc[0:1, C * nxt + 512 * hh: C * nxt + 512 * hh + 512],
                          ksacc[0:1, C * cur + 512 * hh: C * cur + 512 * hh + 512],
                          pp[b][0:1, 0:512], ALU.add))
            ev_ks = em.close("vector", "dve")
            ev_ks_last = ev_ks
            ks_cur = nxt
            BK.rel(sb_[0], [ev_ks])
            BK.rel(sb_[1], [ev_ks])

        a_pref(0)
        a_pref(1)
        for gs in range(2 * nta):
            a_proj(gs)
            if gs >= 1:
                a_kv(gs - 1)
            if gs >= 9 and (gs - 9) % 10 == 0 and (gs - 9) // 10 < 5:
                load_B_weight(bw_names[(gs - 9) // 10])
            if nta < 5 and gs == 2 * nta - 1:
                for n in bw_names[max(0, (gs - 9) // 10 + 1) if gs >= 9 else 0:]:
                    load_B_weight(n)
            if gs % 2 == 1:
                em.flush()
        if nta:
            a_kv(2 * nta - 1)
        em.flush()

        em.wait("vector", ev_kv_last)
        ev_kvsb = em.op("vector", lambda E: E.tensor_copy(kv_sb[:], PKV[:, 0:512]))
        BK.excluded = set()
        BK.rel(6, [ev_kvsb])
        em.wait("vector", ev_ks_last)
        ev_kfb = em.op("vector", lambda E, cur=ks_cur: E.tensor_copy(
            kfb[0:1, :], ksacc[0:1, C * cur: C * cur + C]))
        em.wait("tensor", ev_kfb)
        bb_ = [BK.acq(), BK.acq()]
        for hh in range(2):
            em.op("tensor", lambda E, b=bb_[hh], hh=hh: E.matmul(
                pp[b][:], ones_rb[0:1, 0:128], kfb[0:1, 512 * hh: 512 * hh + 512],
                start=True, stop=True, skip_group_check=True))
        ev_ksb_mm = em.close("tensor", "pe")
        em.wait("vector", ev_ksb_mm)
        for hh in range(2):
            em.op("vector", lambda E, hh=hh, b=bb_[hh]: E.tensor_copy(
                ksumB[:, 512 * hh: 512 * hh + 512], pp[b][:]))
        ev_ksumB = em.close("vector", "dve")
        BK.rel(bb_[0], [ev_ksumB])
        BK.rel(bb_[1], [ev_ksumB])
        em.flush()

        xbfa_cm.__exit__(None, None, None)
        wv_cm.__exit__(None, None, None)
        wk_cm.__exit__(None, None, None)
        ksacc_cm.__exit__(None, None, None)
        kfb_cm.__exit__(None, None, None)
        pe_after_A = ("pe", em.cnt["pe"])

        xf32 = [e(nc.sbuf_tensor(f"xf32_{k}", [128, 2048], F32)) for k in range(2)]
        r1 = e(nc.sbuf_tensor("r1", [128, 2048], F32))
        x1 = [e(nc.sbuf_tensor(f"x1_{k}", [128, 2048], F32)) for k in range(2)]
        rsig = e(nc.sbuf_tensor("rsig", [128, 2048], F32))
        kact = e(nc.sbuf_tensor("kact", [128, 2048], BF16))
        xr = [e(nc.sbuf_tensor(f"xr{k}", [128, 2048], BF16)) for k in range(2)]
        xk = [e(nc.sbuf_tensor(f"xk{k}", [128, 2048], BF16)) for k in range(2)]
        rbf = e(nc.sbuf_tensor("rbf", [128, 2048], BF16))
        rbf2 = e(nc.sbuf_tensor("rbf2", [128, 2048], BF16))
        bcb = e(nc.sbuf_tensor("bcb", [128, 1024], F32))
        rlsc = e(nc.sbuf_tensor("rlsc", [128, 256], F32))
        vq2 = [e(nc.sbuf_tensor(f"vq2_{k}", [128, 2048], BF16)) for k in range(2)]
        ysb = e(nc.sbuf_tensor("ysb", [128, 2048], BF16))

        BK.small = (6, 7)
        BK.qpair = (4, 5)
        evB = {
            "xbf_free": dict(ev_xbf_free),
            "xf_load": {},
            "xbf_cast": {},
            "st1": {},
            "ln2": {},
            "bcb1": None,
            "rbf2_free": None,
            "qsb_free": pe_after_A,
            "kq_free": [pe_after_A, pe_after_A],
            "qT_free": {0: ev_kv_last, 1: ev_kv_last},
            "y_free": None,
            "rec_free": None,
            "tmpf_pool_guard": ("dve", em.cnt["dve"]),
            "x1_free": {0: [pe_after_A], 1: [pe_after_A]},
            "xr_free": {0: None, 1: None},
            "xk_free": {0: None, 1: None},
            "carry_free": {0: None, 1: None},
            "carry_ready": {},
            "stats_free": None,
            "rbf_free": None,
            "rsig_free": pe_after_A,
            "r1_free": pe_after_A,
            "bcb1_free": None,
            "bcb2_free": None,
            "x1_ready": {},
            "xr_ready": {},
            "xk_ready": {},
            "tp_done": {},
            "qsc_done": {},
            "qsc1_pend": {},
            "q_pend": {},
            "p2": {},
            "sd1_ev": None,
            "ln2c": {},
            "qsc1_pend": {},
            "p4b": {},
            "r2a_ev": {},
            "ch1": {},
            "ch2": {},
            "st2": {},
            "mean2_free": None,
            "qsc_pend": {},
            "r1_evts": {},
            "mexp_sm_free": None,
        }

        for t0 in range(min(2, ntb)):
            em.wait("sync", pe_after_A)
            evB["xf_load"][t0] = em.op(
                "sync", lambda E, t0=t0, sl=t0 % 2: E.dma_start(
                    xf32[sl][:], x_tile_ap(xT, t0)), "dxf%d" % (t0 % 2), 16)
        if ntb:
            em.waits("scalar", [evB["xbf_free"][0], evB["xf_load"][0]])
            evB["xbf_cast"][0] = em.op(
                "scalar", lambda E: E.copy(xbf[0][:], xf32[0][:]))

        def part1a_s(i, s):
            sl = i % 2
            if True:
                ev_xbf = evB["xbf_cast"][i] if s == 0 else evB["xbf_cast"].pop(i)
                em.waits("tensor", [ev_xbf, ev_w["wq"], ev_b, ev_ones])
                qb = [BK.acq_q(), BK.acq_q()]
                for hh in range(2):
                    em.op("tensor", lambda E, b=qb[hh], hh=hh: E.matmul(
                        pp[b][:], ones_rb[0:1, 0:128],
                        bqkv[0:1, 512 * hh: 512 * hh + 512],
                        start=True, stop=False, skip_group_check=True))
                for j in range(NCH):
                    for hh in range(2):
                        em.op("tensor", lambda E, b=qb[hh], hh=hh, j=j, s=s, sl=sl: E.matmul(
                            pp[b][:],
                            xbf[0][:, 256 * j + 128 * s: 256 * j + 128 * s + 128],
                            w_sb["wq"][:, C * j + 512 * hh: C * j + 512 * hh + 512],
                            start=False, stop=(j == NCH - 1), skip_group_check=True))
                ev_qp = em.close("tensor", "pe")

                em.waits("scalar", [ev_qp] + [evB["kq_free"][s]])
                for hh in range(2):
                    em.op("scalar", lambda E, hh=hh, s=s, b=qb[hh]: E.activation(
                        kq[:, 1024 * s + 512 * hh: 1024 * s + 512 * hh + 512],
                        pp[b][:], AF.Exp))
                ev_expq = em.close("scalar", "act")
                BK.rel(qb[0], [ev_expq])
                BK.rel(qb[1], [ev_expq])
                if s == NS - 1:
                    evB["xbf_free"][0] = ev_qp
                evB["q_pend"].setdefault(i, []).append((s, ev_expq))

        def part1b_chain(i):
            for s, ev_expq in evB["q_pend"].pop(i):
                kqs = kq[:, 1024 * s: 1024 * s + 1024]
                em.waits("vector", [ev_expq, ev_ksumB])
                em.op("vector", lambda E, kqs=kqs: E.reduce_max(
                    sm16[:, 0:16], kqs.rearrange("p (h d) -> p h d", h=H),
                    axis=AX.X))
                em.op("vector", lambda E, kqs=kqs: E.tensor_tensor(
                    tmpf[:], kqs, ksumB[:], ALU.mult))
                ev_dred = em.op("vector", lambda E: E.reduce_sum(
                    sm16[:, 16:32], tmpf[:].rearrange("p (h d) -> p h d", h=H),
                    axis=AX.X))
                em.op("vector", lambda E: E.scalar_tensor_tensor(
                    sm16[:, 32:48], sm16[:, 0:16], AT_EPS, sm16[:, 16:32],
                    ALU.mult, ALU.add))
                ev_rec = em.op("vector", lambda E, s=s: E.reciprocal(
                    sm16[:, 48 + 16 * s: 64 + 16 * s], sm16[:, 32:48]))
                if s == 0:
                    em.waits("vector", [evB["qsb_free"]])
                    recB = bass.AP(sm16, 48, [[80, 128], [1, H], [0, HD]])
                    ev_qsc = em.op("vector", lambda E, s=s, kqs=kqs, recB=recB:
                                   E.tensor_tensor(
                        qsb[:, 1024 * s: 1024 * s + 1024].rearrange(
                            "p (h d) -> p h d", h=H),
                        kqs.rearrange("p (h d) -> p h d", h=H),
                        recB, ALU.mult))
                    evB["kq_free"][s] = ev_qsc
                else:
                    evB["qsc1_pend"][i] = kqs

        def emit_qsc1():
            if not evB["qsc1_pend"]:
                return
            i, kqs = evB["qsc1_pend"].popitem()
            s = NS - 1
            em.waits("vector", [evB["qsb_free"]])
            recB = bass.AP(sm16, 48 + 16 * s, [[80, 128], [1, H], [0, HD]])
            ev_qsc = em.op("vector", lambda E, s=s, kqs=kqs, recB=recB:
                           E.tensor_tensor(
                qsb[:, 1024 * s: 1024 * s + 1024].rearrange(
                    "p (h d) -> p h d", h=H),
                kqs.rearrange("p (h d) -> p h d", h=H),
                recB, ALU.mult))
            evB["kq_free"][s] = ev_qsc
            evB["qsc_done"][i] = ev_qsc

        def part2_uwo(i):
            sl = i % 2
            ev_xf = evB["xf_load"][i]
            if i + 1 < ntb:
                em.waits("scalar", [evB["xbf_free"][0], evB["xf_load"][i + 1]])
                evB["xbf_cast"][i + 1] = em.op(
                    "scalar", lambda E, sl2=(i + 1) % 2: E.copy(xbf[0][:], xf32[sl2][:]))
            ev_yevac = []
            tpd = evB["tp_done"].pop(i)
            em.waits("tensor", [ev_kvsb, evB["y_free"]])
            for bk in range(4):
                em.waits("tensor", tpd[bk // 2])
                b = BK.acq()
                for pj in (2 * bk, 2 * bk + 1):
                    for r in range(2):
                        h = 2 * pj + r
                        em.op("tensor", lambda E, b=b, pj=pj, r=r, h=h, sl=sl: E.matmul(
                            pp[b][64 * r: 64 * r + 64,
                                  256 * (pj % 2): 256 * (pj % 2) + 256],
                            kv_sb[64 * r: 64 * r + 64,
                                  64 * (h // 2): 64 * (h // 2) + 64],
                            vq2[sl][64 * r: 64 * r + 64,
                                    256 * (h // 2): 256 * (h // 2) + 256],
                            start=True, stop=False, skip_group_check=True))
                ev_u = em.close("tensor", "pe")
                em.waits("vector", [ev_u])
                ev_ye = em.op("vector", lambda E, b=b, bk=bk: E.tensor_copy(
                    ysb[:, 512 * bk: 512 * bk + 512], pp[b][:]))
                ev_yevac.append(ev_ye)
                BK.rel(b, [ev_ye])
            evB["qT_free"][sl] = ("pe", em.cnt["pe"])
            evB["p2"][i] = (ev_yevac, ev_xf, sl)

        def part2_wo(i):
            ev_yevac, ev_xf, sl = evB["p2"].pop(i)
            em.waits("tensor", [ev_w["wo"]])
            ev_r1 = {}
            for ec in range(NCH):
                wob = BK.acq()
                for cj in range(NCH):
                    em.wait("tensor", ev_yevac[cj // 2])
                    em.op("tensor", lambda E, b=wob, ec=ec, cj=cj: E.matmul(
                        pp[b][:, 0:256],
                        w_sb["wo"][:, C * cj + 128 * ec: C * cj + 128 * ec + 128],
                        ysb[:, 256 * cj: 256 * cj + 256],
                        start=(cj == 0), stop=(cj == NCH - 1), skip_group_check=True))
                ev_wo = em.close("tensor", "pe")
                em.waits("vector", [ev_wo, ev_xf, ld_ms, evB["r1_free"]])
                ev_r1[ec] = em.op("vector", lambda E, b=wob, ec=ec, sl=sl:
                                  E.scalar_tensor_tensor(
                    r1[:, 256 * ec: 256 * ec + 256],
                    pp[b][:, 0:256],
                    bo8[:, ec: ec + 1],
                    xf32[sl][:, 256 * ec: 256 * ec + 256],
                    ALU.add, ALU.add))
                BK.rel(wob, [ev_r1[ec]])
            evB["y_free"] = ("pe", em.cnt["pe"])
            evB["r1_evts"][i] = ev_r1
            em.wait("scalar", evB["rbf_free"])
            ev_r1bf = {}
            for ec in range(NCH):
                em.wait("scalar", ev_r1[ec])
                ev_r1bf[ec] = em.op("scalar", lambda E, ec=ec: E.copy(
                    rbf[:, 256 * ec: 256 * ec + 256],
                    r1[:, 256 * ec: 256 * ec + 256]))
            stb = BK.acq_small()
            for ec in range(NCH):
                em.wait("tensor", ev_r1bf[ec])
                em.op("tensor", lambda E, b=stb, ec=ec: E.matmul(
                    pp[b][0:1, 0:256], ones_cb[:, 0:1],
                    rbf[:, 256 * ec: 256 * ec + 256],
                    start=(ec == 0), stop=(ec == NCH - 1), skip_group_check=True))
            ev_sum1 = em.close("tensor", "pe")
            em.wait("scalar", ev_sum1)
            ev_sq1a = em.op("scalar", lambda E: E.square(
                rbf[:, 0:1024], r1[:, 0:1024]))
            ev_sq1b = em.op("scalar", lambda E: E.square(
                rbf[:, 1024:2048], r1[:, 1024:2048]))
            for ec in range(NCH):
                em.wait("tensor", ev_sq1a if ec < 4 else ev_sq1b)
                em.op("tensor", lambda E, b=stb, ec=ec: E.matmul(
                    pp[b][0:1, 256:512], ones_cb[:, 0:1],
                    rbf[:, 256 * ec: 256 * ec + 256],
                    start=(ec == 0), stop=(ec == NCH - 1), skip_group_check=True))
            ev_stats1 = em.close("tensor", "pe")
            evB["rbf_free"] = ev_stats1
            evB["st1"][i] = (stb, ev_stats1)
            evB["xf_load"].pop(i)
            if i + 2 < ntb:
                em.wait("sync", ev_r1[NCH - 1])
                evB["xf_load"][i + 2] = em.op(
                    "sync", lambda E, i=i, sl=sl: E.dma_start(
                        xf32[sl][:], x_tile_ap(xT, i + 2)), "dxf%d" % sl, 16)

        def part4a_chain1(i):
            stb, ev_stats1 = evB["st1"].pop(i)

            em.waits("vector", [ev_stats1, evB["stats_free"]])
            em.op("vector", lambda E, b=stb: E.tensor_scalar(
                stats[0:1, 0:256], pp[b][0:1, 0:256], 1.0 / C, None, ALU.mult))
            em.op("vector", lambda E, b=stb: E.tensor_scalar(
                stats[0:1, 256:512], pp[b][0:1, 256:512], 1.0 / C, None, ALU.mult))
            ev_ms = em.close("vector", "dve")
            BK.rel(stb, [ev_ms])
            em.op("vector", lambda E: E.tensor_tensor(
                stats[0:1, 512:768], stats[0:1, 0:256], stats[0:1, 0:256], ALU.mult))
            ev_var = em.op("vector", lambda E: E.tensor_tensor(
                stats[0:1, 768:1024], stats[0:1, 256:512], stats[0:1, 512:768],
                ALU.subtract))
            em.wait("scalar", ev_var)
            em.op("scalar", lambda E: E.activation(
                stats[0:1, 512:768], stats[0:1, 768:1024], AF.Ln,
                bias=cst[0:1, 0:1]))
            ev_sd = em.op("scalar", lambda E: E.activation(
                stats[0:1, 256:512], stats[0:1, 512:768], AF.Exp, scale=-0.5))
            evB["sd1_ev"] = ev_sd
            evB["ch1"][i] = (ev_sd, ev_ms)

        def part4a_bc1(i):
            ev_sd, ev_ms = evB["ch1"].pop(i)
            em.waits("tensor", [ev_sd, ev_ms])
            bcbk = BK.acq_small()
            em.op("tensor", lambda E, b=bcbk: E.matmul(
                pp[b][:, 0:256], ones_rf[0:1, :], stats[0:1, 256:512],
                start=True, stop=False, skip_group_check=True))
            em.op("tensor", lambda E, b=bcbk: E.matmul(
                pp[b][:, 256:512], nones_rf[0:1, :], stats[0:1, 0:256],
                start=True, stop=True, skip_group_check=True))
            ev_bc1 = em.close("tensor", "pe")
            evB["stats_free"] = ev_bc1
            em.waits("vector", [ev_bc1, evB["bcb1_free"]])
            ev_bcb1 = em.op("vector", lambda E, b=bcbk: E.tensor_copy(
                bcb[:, 0:512], pp[b][:]))
            BK.rel(bcbk, [ev_bcb1])
            evB["bcb1"] = ev_bcb1

        def part4b_norm1mix(i):
            sl = i % 2
            ev_r1 = evB["r1_evts"].pop(i)
            ev_bcb1 = evB["bcb1"]
            em.waits("gpsimd", [ev_bcb1, ev_r1[NCH - 1]])
            em.waits("gpsimd", evB["x1_free"][sl])
            rstdB = bass.AP(bcb, 0, [[1024, 128], [0, NCH], [1, 256]])
            negmB = bass.AP(bcb, 256, [[1024, 128], [0, NCH], [1, 256]])
            em.op("gpsimd", lambda E, sl=sl, negmB=negmB: E.tensor_tensor(
                x1[sl][:].rearrange("p (j t) -> p j t", j=NCH),
                r1[:].rearrange("p (j t) -> p j t", j=NCH), negmB, ALU.add))
            ev_x1 = em.op("gpsimd", lambda E, sl=sl, rstdB=rstdB: E.tensor_tensor(
                x1[sl][:].rearrange("p (j t) -> p j t", j=NCH),
                x1[sl][:].rearrange("p (j t) -> p j t", j=NCH), rstdB, ALU.mult))
            evB["x1_ready"][i] = ev_x1
            evB["r1_free"] = ev_x1
            evB["bcb1_free"] = ev_x1

            wh = (i + 1) % 2
            em.waits("vector", [ev_x1] + (evB["carry_free"][wh] or []))
            for j in range(NCH):
                em.op("vector", lambda E, j=j, wh=wh, sl=sl: E.tensor_copy(
                    carry[:, 8 * wh + j: 8 * wh + j + 1],
                    x1[sl][:, 256 * j + 255: 256 * j + 256]))
            ev_carry = em.close("vector", "dve")
            evB["carry_ready"][i + 1] = ev_carry

            rh = i % 2
            ev_cp = evB["carry_ready"].pop(i, ev_muc)
            em.waits("gpsimd", [ev_x1, ev_cp, ld_ms, evB["xr_free"][sl],
                                evB["xk_free"][sl]])
            crv = carry[:, 8 * rh: 8 * rh + 8].rearrange("p (j o) -> p j o", o=1)
            x1v = x1[sl][:].rearrange("p (j t) -> p j t", j=NCH)
            em.op("gpsimd", lambda E, sl=sl: E.tensor_tensor(
                rbf[:, 1:2048], x1[sl][:, 1:2048], x1[sl][:, 0:2047], ALU.subtract))
            em.op("gpsimd", lambda E, x1v=x1v, crv=crv: E.tensor_tensor(
                rbf[:].rearrange("p (j t) -> p j t", j=NCH)[:, :, 0:1],
                x1v[:, :, 0:1], crv, ALU.subtract))
            for dst, mm in ((xr, mur), (xk, muk)):
                muB = bass.AP(mm, 0, [[8, 128], [1, NCH], [0, 256]])
                em.op("gpsimd", lambda E, dst=dst, sl=sl, muB=muB: E.tensor_tensor(
                    dst[sl][:].rearrange("p (j t) -> p j t", j=NCH),
                    rbf[:].rearrange("p (j t) -> p j t", j=NCH), muB, ALU.mult))
                em.op("gpsimd", lambda E, dst=dst, sl=sl, x1v=x1v: E.tensor_tensor(
                    dst[sl][:].rearrange("p (j t) -> p j t", j=NCH)[:, :, 1:256],
                    dst[sl][:].rearrange("p (j t) -> p j t", j=NCH)[:, :, 1:256],
                    x1v[:, :, 0:255], ALU.add))
                em.op("gpsimd", lambda E, dst=dst, sl=sl, crv=crv: E.tensor_tensor(
                    dst[sl][:].rearrange("p (j t) -> p j t", j=NCH)[:, :, 0:1],
                    dst[sl][:].rearrange("p (j t) -> p j t", j=NCH)[:, :, 0:1],
                    crv, ALU.add))
            ev_mix = em.close("gpsimd", "pool")
            evB["rbf_free"] = [evB["rbf_free"], ev_mix] \
                if evB["rbf_free"] else [ev_mix]
            evB["xr_ready"][i] = ev_mix
            evB["xk_ready"][i] = ev_mix
            evB["carry_free"][rh] = [ev_mix]

        def part3_ffn(i):
            sl = i % 2
            em.waits("tensor", [evB["xk_ready"].pop(i), ev_w["wk2"]])
            ev_kact = None
            for dc in range(NCH):
                kb_ = BK.acq()
                for cj in range(NCH):
                    em.op("tensor", lambda E, b=kb_, dc=dc, cj=cj, sl=sl: E.matmul(
                        pp[b][:, 0:256],
                        w_sb["wk2"][:, C * cj + 128 * dc: C * cj + 128 * dc + 128],
                        xk[sl][:, 256 * cj: 256 * cj + 256],
                        start=(cj == 0), stop=(cj == NCH - 1), skip_group_check=True))
                ev_kp = em.close("tensor", "pe")
                em.wait("scalar", ev_kp)
                ev_rl = em.op("scalar", lambda E, b=kb_, dc=dc: E.activation(
                    rlsc[:, 0:256], pp[b][:, 0:256], AF.Relu))
                ev_kact = em.op("scalar", lambda E, dc=dc: E.square(
                    kact[:, 256 * dc: 256 * dc + 256], rlsc[:, 0:256]))
                BK.rel(kb_, [ev_rl])
            evB["xk_free"][sl] = ("pe", em.cnt["pe"])

            em.waits("tensor", [evB["xr_ready"].pop(i), ev_w["wr"]])
            ev_sig = {}
            for dc in range(NCH):
                rb_ = BK.acq()
                for cj in range(NCH):
                    em.op("tensor", lambda E, b=rb_, dc=dc, cj=cj, sl=sl: E.matmul(
                        pp[b][:, 0:256],
                        w_sb["wr"][:, C * cj + 128 * dc: C * cj + 128 * dc + 128],
                        xr[sl][:, 256 * cj: 256 * cj + 256],
                        start=(cj == 0), stop=(cj == NCH - 1), skip_group_check=True))
                ev_rp = em.close("tensor", "pe")
                em.waits("scalar", [ev_rp, evB["rsig_free"]])
                ev_sig[dc] = em.op("scalar", lambda E, b=rb_, dc=dc: E.activation(
                    rsig[:, 256 * dc: 256 * dc + 256],
                    pp[b][:, 0:256], AF.Sigmoid))
                BK.rel(rb_, [ev_sig[dc]])
            evB["xr_free"][sl] = ("pe", em.cnt["pe"])

            em.waits("tensor", [ev_kact, ev_w["wv2"]])
            ev_r2 = {}
            for ec in range(NCH):
                vb_ = BK.acq_ffn()
                for cj in range(NCH):
                    em.op("tensor", lambda E, b=vb_, ec=ec, cj=cj: E.matmul(
                        pp[b][:, 0:256],
                        w_sb["wv2"][:, C * cj + 128 * ec: C * cj + 128 * ec + 128],
                        kact[:, 256 * cj: 256 * cj + 256],
                        start=(cj == 0), stop=(cj == NCH - 1), skip_group_check=True))
                ev_vp = em.close("tensor", "pe")
                em.waits("vector", [ev_vp, ev_sig[ec]])
                ev_r2[ec] = em.op("vector", lambda E, b=vb_, ec=ec: E.tensor_tensor(
                    rsig[:, 256 * ec: 256 * ec + 256],
                    rsig[:, 256 * ec: 256 * ec + 256],
                    pp[b][:, 0:256], ALU.mult))
                BK.rel(vb_, [ev_r2[ec]])
                if ec == 1:
                    emit_qsc1()
            em.waits("vector", [evB["x1_ready"][i]])
            ev_r2a_a = em.op("vector", lambda E, sl=sl: E.tensor_tensor(
                rsig[:, 0:1024], rsig[:, 0:1024], x1[sl][:, 0:1024], ALU.add))
            em.waits("scalar", [ev_r2a_a, evB["rbf2_free"]])
            ev_r2bf_a = em.op("scalar", lambda E: E.copy(
                rbf2[:, 0:1024], rsig[:, 0:1024]))
            ev_r2a = em.op("vector", lambda E, sl=sl: E.tensor_tensor(
                rsig[:, 1024:2048], rsig[:, 1024:2048], x1[sl][:, 1024:2048],
                ALU.add))
            em.wait("scalar", ev_r2a)
            ev_r2bf = em.op("scalar", lambda E: E.copy(
                rbf2[:, 1024:2048], rsig[:, 1024:2048]))
            evB["x1_free"][sl] = [ev_r2a]
            evB["x1_ready"].pop(i)
            evB["r2a_ev"][i] = ev_r2a
            evB["ln2"][i] = (ev_r2bf_a, ev_r2bf, ev_r2a)

        def part3b_pre(i):
            ev_r2bf_a, ev_r2bf, ev_r2a = evB["ln2"].pop(i)
            stb = BK.acq_small()
            for ec in range(NCH):
                em.wait("tensor", ev_r2bf_a if ec < 4 else ev_r2bf)
                em.op("tensor", lambda E, b=stb, ec=ec: E.matmul(
                    pp[b][0:1, 0:256], ones_cb[:, 0:1],
                    rbf2[:, 256 * ec: 256 * ec + 256],
                    start=(ec == 0), stop=(ec == NCH - 1), skip_group_check=True))
            ev_sum2 = em.close("tensor", "pe")
            em.wait("scalar", ev_sum2)
            ev_sq2 = em.op("scalar", lambda E: E.square(rbf2[:], rsig[:]))
            em.waits("vector", [ev_sum2, evB["mean2_free"]])
            ev_m2 = em.op("vector", lambda E, b=stb: E.tensor_scalar(
                stats[0:1, 1024:1280], pp[b][0:1, 0:256], 1.0 / C, None, ALU.mult))
            BK.rel(stb, [ev_m2])
            evB["st2"][i] = (ev_m2, ev_sq2, ev_r2a)

        def part3b_ln2mm(i):
            ev_m2, ev_sq2, ev_r2a = evB["st2"].pop(i)
            stb = BK.acq_small()
            em.wait("tensor", ev_sq2)
            for ec in range(NCH):
                em.op("tensor", lambda E, b=stb, ec=ec: E.matmul(
                    pp[b][0:1, 256:512], ones_cb[:, 0:1],
                    rbf2[:, 256 * ec: 256 * ec + 256],
                    start=(ec == 0), stop=(ec == NCH - 1), skip_group_check=True))
            ev_stats2 = em.close("tensor", "pe")
            evB["rbf2_free"] = ev_stats2

            em.waits("vector", [ev_stats2, ev_m2])
            em.op("vector", lambda E, b=stb: E.tensor_scalar(
                stats[0:1, 1280:1536], pp[b][0:1, 256:512], 1.0 / C, None, ALU.mult))
            ev_ms = em.close("vector", "dve")
            BK.rel(stb, [ev_ms])
            evB["ln2c"][i] = (ev_ms, ev_m2, ev_r2a, ev_sq2)

        def part3b_ln2(i):
            ev_ms, ev_m2, ev_r2a, ev_sq2 = evB["ln2c"].pop(i)
            em.waits("vector", [ev_ms, ev_m2])
            em.op("vector", lambda E: E.tensor_tensor(
                stats[0:1, 1536:1792], stats[0:1, 1024:1280], stats[0:1, 1024:1280],
                ALU.mult))
            em.wait("vector", evB["sd1_ev"])
            ev_var = em.op("vector", lambda E: E.tensor_tensor(
                stats[0:1, 512:768], stats[0:1, 1280:1536], stats[0:1, 1536:1792],
                ALU.subtract))
            em.wait("scalar", ev_var)
            em.op("scalar", lambda E: E.activation(
                stats[0:1, 1536:1792], stats[0:1, 512:768], AF.Ln,
                bias=cst[0:1, 0:1]))
            ev_sd = em.op("scalar", lambda E: E.activation(
                stats[0:1, 512:768], stats[0:1, 1536:1792], AF.Exp, scale=-0.5))
            evB["ch2"][i] = (ev_sd, ev_ms, ev_r2a, ev_sq2)

        def part3b_bc2(i):
            ev_sd, ev_ms, ev_r2a, ev_sq2 = evB["ch2"].pop(i)
            em.waits("tensor", [ev_sd, ev_ms])
            bcbk = BK.acq_small()
            em.op("tensor", lambda E, b=bcbk: E.matmul(
                pp[b][:, 0:256], ones_rf[0:1, :], stats[0:1, 512:768],
                start=True, stop=False, skip_group_check=True))
            em.op("tensor", lambda E, b=bcbk: E.matmul(
                pp[b][:, 256:512], nones_rf[0:1, :], stats[0:1, 1024:1280],
                start=True, stop=True, skip_group_check=True))
            ev_bc2 = em.close("tensor", "pe")
            evB["stats_free"] = ev_bc2
            evB["mean2_free"] = ev_bc2
            em.waits("vector", [ev_bc2, evB["bcb2_free"]])
            ev_bcb2 = em.op("vector", lambda E, b=bcbk: E.tensor_copy(
                bcb[:, 512:1024], pp[b][:]))
            BK.rel(bcbk, [ev_bcb2])

            em.waits("gpsimd", [ev_bcb2, ev_r2a, ev_sq2])
            rstdB2 = bass.AP(bcb, 512, [[1024, 128], [0, NCH], [1, 256]])
            negmB2 = bass.AP(bcb, 768, [[1024, 128], [0, NCH], [1, 256]])
            em.op("gpsimd", lambda E, negmB2=negmB2: E.tensor_tensor(
                rsig[:].rearrange("p (j t) -> p j t", j=NCH),
                rsig[:].rearrange("p (j t) -> p j t", j=NCH), negmB2, ALU.add))
            ev_out = em.op("gpsimd", lambda E, rstdB2=rstdB2: E.tensor_tensor(
                rsig[:].rearrange("p (j t) -> p j t", j=NCH),
                rsig[:].rearrange("p (j t) -> p j t", j=NCH), rstdB2, ALU.mult))
            evB["r2_free"] = ev_out
            evB["bcb2_free"] = ev_out

            em.wait("sync", ev_out)
            half = bass.AP(outT, T * i, [[S, 128], [128 * S, 4], [1, T]])
            half2 = bass.AP(outT, T * i + 4 * 128 * S, [[S, 128], [128 * S, 4], [1, T]])
            ev_s1 = em.op("sync", lambda E, half=half: E.dma_start(
                half, rsig[:, 0:1024]), "dst", 16)
            ev_s2 = em.op("sync", lambda E, half2=half2: E.dma_start(
                half2, rsig[:, 1024:2048]), "dst2", 16)
            evB["rsig_free"] = [ev_s1, ev_s2]

        def part5_tp(i):
            sl = i % 2
            emit_qsc1()
            em.waits("tensor", [evB["qsc_done"].pop(i), evB["qT_free"][sl], ev_id])
            evs = {0: [], 1: []}
            for g, (jg, s) in enumerate([(0, 0), (0, 1), (1, 0), (1, 1)]):
                b = BK.acq()
                for blk in range(4):
                    j = 4 * jg + blk
                    em.op("tensor", lambda E, b=b, s=s, j=j, blk=blk: E.matmul(
                        pp[b][:, 128 * blk: 128 * blk + 128],
                        qsb[:, 1024 * s + 128 * j: 1024 * s + 128 * j + 128],
                        id_sb[:],
                        start=True, stop=False, skip_group_check=True))
                ev_mm = em.close("tensor", "pe")
                dst = bass.AP(vq2[sl], 1024 * jg + 128 * s,
                              [[2048, 128], [256, 4], [1, 128]])
                if g % 2 == 0:
                    em.wait("vector", ev_mm)
                    ev_e = em.op("vector", lambda E, b=b, dst=dst: E.tensor_copy(
                        dst, pp[b][:]))
                else:
                    em.wait("scalar", ev_mm)
                    ev_e = em.op("scalar", lambda E, b=b, dst=dst: E.copy(
                        dst, pp[b][:]))
                evs[jg].append(ev_e)
                BK.rel(b, [ev_e])
            evB["qsb_free"] = ev_mm
            evB["tp_done"][i] = evs

        for it in range(ntb + 3):
            if 3 <= it <= ntb + 2:
                part3b_pre(it - 3)
            if 1 <= it <= ntb:
                part2_uwo(it - 1)
                part2_wo(it - 1)
            if 3 <= it <= ntb + 2:
                part3b_ln2mm(it - 3)
            if 1 <= it <= ntb:
                part4a_chain1(it - 1)
            if 3 <= it <= ntb + 2:
                part3b_ln2(it - 3)
            if it < ntb:
                part1a_s(it, 0)
            if 1 <= it <= ntb:
                part4a_bc1(it - 1)
            if 3 <= it <= ntb + 2:
                part3b_bc2(it - 3)
            if it < ntb:
                part1a_s(it, 1)
                part1b_chain(it)
            if 1 <= it <= ntb:
                part4b_norm1mix(it - 1)
            if 2 <= it <= ntb + 1:
                part3_ffn(it - 2)
            if it < ntb:
                part5_tp(it)
            em.flush()

        if dbg:
            kvdbg = nc.dram_tensor("kvdbg", [128, 512], F32, kind="ExternalOutput").ap()
            ksdbg = nc.dram_tensor("ksdbg", [128, C], F32, kind="ExternalOutput").ap()
            em.waits("gpsimd", [ev_kvsb, ev_ksumB])
            em.op("gpsimd", lambda E: E.dma_start(kvdbg, kv_sb[:]), "ddbg", 16)
            em.op("gpsimd", lambda E: E.dma_start(ksdbg, ksumB[:]), "ddbg", 16)
            em.raw("gpsimd", lambda E, v=em.cnt["ddbg"]: E.wait_ge(sems["ddbg"], v))
        v_final = em.cnt["dst"]
        em.raw("scalar", lambda E, v=v_final: E.wait_ge(sems["dst"], v))
        v_final2 = em.cnt["dst2"]
        em.raw("scalar", lambda E, v=v_final2: E.wait_ge(sems["dst2"], v))
        em.flush()

    build_program.last_em = em
    return nc


def kernel(**inputs):
    x = np.asarray(inputs["L0_attn_x"], dtype=np.float32)
    f32c = lambda a: np.ascontiguousarray(np.asarray(a, dtype=np.float32))
    wT = {
        "wq": f32c(np.asarray(inputs["L0_attn_Wq"]).T),
        "wk": f32c(np.asarray(inputs["L0_attn_Wk"]).T),
        "wv": f32c(np.asarray(inputs["L0_attn_Wv"]).T),
        "wo": f32c(np.asarray(inputs["L0_attn_Wo"]).T),
        "wr": f32c(np.asarray(inputs["L0_ffn_W_r"]).T),
        "wk2": f32c(np.asarray(inputs["L0_ffn_W_k"]).T),
        "wv2": f32c(np.asarray(inputs["L0_ffn_W_v"]).T),
    }
    bqkv = f32c(np.concatenate([np.asarray(inputs["L0_attn_bq"]),
                                np.asarray(inputs["L0_attn_bk"]),
                                np.asarray(inputs["L0_attn_bv"])]).reshape(1, 3 * C))
    bo8 = f32c(np.asarray(inputs["L0_attn_bo"]).reshape(8, 128).T)
    mur8 = f32c(np.asarray(inputs["L0_ffn_mu_r"]).reshape(8, 128).T)
    muk8 = f32c(np.asarray(inputs["L0_ffn_mu_k"]).reshape(8, 128).T)

    nc = build_program()
    in_maps = []
    for b in range(8):
        m = {"xT": f32c(x[b].T), "bqkv": bqkv, "bo8": bo8,
             "mur8": mur8, "muk8": muk8, "id128": np.eye(128, dtype=np.float32)}
        m.update(wT)
        in_maps.append(m)
    res = run_bass_kernel_spmd(nc, in_maps, core_ids=list(range(8)))
    out = np.stack([np.asarray(res.results[b]["outT"]).T for b in range(8)], axis=0)
    return np.ascontiguousarray(out, dtype=np.float32)


if __name__ == "__main__":
    if "--build" in sys.argv:
        import time
        t0 = time.time()
        nc = build_program()
        print(f"build ok in {time.time() - t0:.1f}s")



# version 76
# speedup vs baseline: 1.0430x; 1.0430x over previous
import sys
sys.path.insert(0, "/opt/trn_rl_repo")
from contextlib import ExitStack

import sys
import numpy as np
import concourse.bass as bass
import concourse.mybir as mybir
from concourse.bass_utils import run_bass_kernel_spmd

F32 = mybir.dt.float32
BF16 = mybir.dt.bfloat16
AF = mybir.ActivationFunctionType
ALU = mybir.AluOpType
AX = mybir.AxisListType

S = 8192
C = 1024
H = 16
HD = 64
T = 256
NT = S // T
NS = T // 128
NCH = C // 128
LN_EPS = 1e-5
AT_EPS = 1e-6

ENGINES = ["sync", "scalar", "vector", "tensor", "gpsimd"]


class Em:

    OWN = {"vector": "dve", "scalar": "act", "gpsimd": "pool"}

    def __init__(self, block, sems):
        self.block = block
        self.sems = sems
        self.log = {e: [] for e in ENGINES}
        self.cnt = {s: 0 for s in sems}
        self.lastw = {}
        self.last_evt = {}
        self.pend = {e: [] for e in ENGINES}

    def wait(self, eng, evt):
        if evt is None:
            return
        if isinstance(evt, list):
            for e2 in evt:
                self.wait(eng, e2)
            return
        sem, v = evt
        k = (eng, sem)
        if self.lastw.get(k, 0) >= v:
            return
        self.lastw[k] = v
        h = self.sems[sem]
        self.pend[eng].append(lambda E, h=h, v=v: E.wait_ge(h, v))

    def waits(self, eng, evts):
        for e in evts:
            self.wait(eng, e)

    def raw(self, eng, fn):
        self.pend[eng].append(fn)

    def op(self, eng, fn, sem=None, step=1):
        self.log[eng].append((sys._getframe(1).f_lineno,
                              "dma" if (sem is not None and step == 16) else "op"))
        own = self.OWN.get(eng)
        if own is not None and not (sem is not None and step == 16):
            prev = self.cnt[own]
            if prev > self.lastw.get((eng, own), 0):
                self.lastw[(eng, own)] = prev
                hw = self.sems[own]
                self.pend[eng].append(lambda E, h=hw, v=prev: E.wait_ge(h, v))
            self.cnt[own] += 1
            v = self.cnt[own]
            h = self.sems[own]
            self.pend[eng].append(lambda E, fn=fn, h=h: fn(E).then_inc(h, 1))
            self.last_evt[eng] = (own, v)
            return (own, v)
        if sem is None:
            self.pend[eng].append(fn)
            return None
        self.cnt[sem] += step
        v = self.cnt[sem]
        h = self.sems[sem]
        self.pend[eng].append(lambda E, fn=fn, h=h, step=step: fn(E).then_inc(h, step))
        return (sem, v)

    def close(self, eng, sem, step=1):
        if eng in self.OWN:
            return self.last_evt[eng]
        fn = self.pend[eng].pop()
        self.cnt[sem] += step
        v = self.cnt[sem]
        h = self.sems[sem]
        self.pend[eng].append(lambda E, fn=fn, h=h, step=step: fn(E).then_inc(h, step))
        return (sem, v)

    def flush(self):
        for eng in ENGINES:
            ops = self.pend[eng]
            if not ops:
                continue
            self.pend[eng] = []

            def body(E, ops=ops):
                for f in ops:
                    f(E)
            getattr(self.block, eng)(body)


class BankPool:
    def __init__(self, em, pp):
        self.em = em
        self.pp = pp
        self.free_evt = [None] * 8
        self.ptr = 0
        self.sptr = 0
        self.qptr = 0
        self.fptr = 0
        self.excluded = set()
        self.small = ()
        self.qpair = ()

    def _take(self, b):
        if self.free_evt[b]:
            for evt in self.free_evt[b]:
                self.em.wait("tensor", evt)
        self.free_evt[b] = None
        return b

    def acq(self):
        for _ in range(16):
            b = self.ptr % 8
            self.ptr += 1
            if b in self.excluded or b in self.small:
                continue
            return self._take(b)
        raise RuntimeError("no psum bank available")

    def acq_small(self):
        if not self.small:
            return self.acq()
        b = self.small[self.sptr % len(self.small)]
        self.sptr += 1
        return self._take(b)

    def acq_q(self):
        if not self.qpair:
            return self.acq()
        b = self.qpair[self.qptr % len(self.qpair)]
        self.qptr += 1
        return self._take(b)

    def acq_ffn(self):
        pool = (0, 1, 2, 3, 6, 7)
        b = pool[self.fptr % len(pool)]
        self.fptr += 1
        return self._take(b)

    def rel(self, b, evts):
        self.free_evt[b] = [e for e in evts if e is not None]


def build_program(nta=NT, ntb=NT, dbg=False):
    nc = bass.Bass("TRN2", target_bir_lowering=False)

    xT = nc.dram_tensor("xT", [C, S], F32, kind="ExternalInput")
    outT = nc.dram_tensor("outT", [C, S], F32, kind="ExternalOutput")
    WNAMES = ["wq", "wk", "wv", "wo", "wr", "wk2", "wv2"]
    wd = {n: nc.dram_tensor(n, [C, C], F32, kind="ExternalInput") for n in WNAMES}
    bqkv_d = nc.dram_tensor("bqkv", [1, 3 * C], F32, kind="ExternalInput")
    bo8_d = nc.dram_tensor("bo8", [128, 8], F32, kind="ExternalInput")
    mur_d = nc.dram_tensor("mur8", [128, 8], F32, kind="ExternalInput")
    muk_d = nc.dram_tensor("muk8", [128, 8], F32, kind="ExternalInput")
    id_d = nc.dram_tensor("id128", [128, 128], F32, kind="ExternalInput")

    def x_tile_ap(dram, i):
        return bass.AP(dram, T * i, [[S, 128], [128 * S, 8], [1, T]])

    def w_ap(dram):
        return bass.AP(dram, 0, [[C, 128], [128 * C, 8], [1, C]])

    with ExitStack() as st:
        e = st.enter_context
        w_sb = {}
        for n in ["wq", "wo", "wr", "wk2", "wv2"]:
            w_sb[n] = e(nc.sbuf_tensor("w_" + n, [128, 8 * C], BF16))

        bqkv = e(nc.sbuf_tensor("sb_bqkv", [1, 3 * C], BF16))
        bo8 = e(nc.sbuf_tensor("sb_bo8", [128, 8], F32))
        mur = e(nc.sbuf_tensor("mur", [128, 8], F32))
        muk = e(nc.sbuf_tensor("muk", [128, 8], F32))
        murc = e(nc.sbuf_tensor("murc", [128, 8], F32))
        mukc = e(nc.sbuf_tensor("mukc", [128, 8], F32))
        ones_rb = e(nc.sbuf_tensor("ones_rb", [1, 256], BF16))
        zrow = e(nc.sbuf_tensor("zrow", [1, 128], BF16))
        ones_cb = e(nc.sbuf_tensor("ones_cb", [128, 1], BF16))
        ones_rf = e(nc.sbuf_tensor("ones_rf", [1, 128], F32))
        nones_rf = e(nc.sbuf_tensor("nones_rf", [1, 128], F32))
        ksumB = e(nc.sbuf_tensor("ksumB", [128, C], BF16))
        kv_sb = e(nc.sbuf_tensor("kv_sb", [128, 512], BF16))
        stats = e(nc.sbuf_tensor("stats", [1, 1792], F32))
        sm16 = e(nc.sbuf_tensor("sm16", [128, 80], F32))
        carry = e(nc.sbuf_tensor("carry", [128, 16], F32))
        cst = e(nc.sbuf_tensor("cst", [1, 4], F32))
        id_sb = e(nc.sbuf_tensor("id_sb", [128, 128], BF16))

        xbf = [e(nc.sbuf_tensor("xbf0", [128, 2048], BF16))]
        tmpf = e(nc.sbuf_tensor("tmpf", [128, 1024], F32))
        kq = e(nc.sbuf_tensor("kq", [128, 2048], BF16))
        qsb = e(nc.sbuf_tensor("qsb", [128, 2048], BF16))

        kfb_cm = nc.sbuf_tensor("kfb", [1, C], BF16)
        ksacc_cm = nc.sbuf_tensor("ksacc", [1, 2 * C], F32)
        wk_cm = nc.sbuf_tensor("w_wk", [128, 8 * C], BF16)
        wv_cm = nc.sbuf_tensor("w_wv", [128, 8 * C], BF16)
        xbfa_cm = nc.sbuf_tensor("xbf_a", [128, 2048], BF16)
        kfb = kfb_cm.__enter__()
        ksacc = ksacc_cm.__enter__()
        w_sb["wk"] = wk_cm.__enter__()
        w_sb["wv"] = wv_cm.__enter__()
        xbf_a = xbfa_cm.__enter__()

        pp = [e(nc.psum_tensor(f"pp{k}", [128, 512], F32)) for k in range(8)]

        semnames = ["pe", "dve", "act", "pool", "dxb0", "dxb1", "dxf0",
                    "dxf1", "dtp", "dst", "dst2", "dwA", "dwB", "dms", "ddbg",
                    "dwk0", "dwk1", "dwk2", "dwk3", "dwv0", "dwv1"]
        sems = {s: e(nc.semaphore("sem_" + s)) for s in semnames}

        block = e(nc.Block())
        em = Em(block, sems)
        BK = BankPool(em, pp)

        em.op("gpsimd", lambda E: E.memset(ones_rb[:], 1.0))
        em.op("gpsimd", lambda E: E.memset(zrow[:], 0.0))
        em.op("gpsimd", lambda E: E.memset(ones_cb[:], 1.0))
        em.op("gpsimd", lambda E: E.memset(ones_rf[:], 1.0))
        em.op("gpsimd", lambda E: E.memset(nones_rf[:], -1.0))
        em.op("gpsimd", lambda E: E.memset(carry[:], 0.0))
        em.op("gpsimd", lambda E: E.memset(cst[0:1, 0:1], LN_EPS))
        ev_init = em.op("gpsimd", lambda E: E.memset(ksacc[:], 0.0))
        ev_ones = ev_init

        ev_w = {}
        ev_wk = []
        for m in range(4):
            ap = bass.AP(wd["wk"], 128 * C * 2 * m, [[C, 128], [128 * C, 2], [1, C]])
            ev_wk.append(em.op("gpsimd", lambda E, m=m, ap=ap: E.dma_start(
                w_sb["wk"][:, 2 * C * m: 2 * C * (m + 1)], ap), "dwk%d" % m, 16))
        ev_wv = []
        for m in range(2):
            ap = bass.AP(wd["wv"], 128 * C * 4 * m, [[C, 128], [128 * C, 4], [1, C]])
            ev_wv.append(em.op("gpsimd", lambda E, m=m, ap=ap: E.dma_start(
                w_sb["wv"][:, 4 * C * m: 4 * C * (m + 1)], ap), "dwv%d" % m, 16))
        ev_b = em.op("gpsimd", lambda E: E.dma_start(bqkv[:], bqkv_d.ap()), "dwA", 16)
        ev_w["wk"] = ev_wk[-1]
        ev_w["wv"] = ev_wv[-1]
        ev_id = em.op("gpsimd", lambda E: E.dma_start(id_sb[:], id_d.ap()), "ddbg", 16)
        bw_names = ["wq", "wo", "wr", "wk2", "wv2"]

        def load_B_weight(n):
            em.op("gpsimd", lambda E, n=n: E.dma_start(w_sb[n][:], w_ap(wd[n])),
                  "dwB", 16)
        for n in ["wq", "wo", "wr", "wk2", "wv2"]:
            ev_w[n] = ("dwB", 80)
        em.op("sync", lambda E: E.dma_start(bo8[:], bo8_d.ap()), "dms", 16)
        em.op("sync", lambda E: E.dma_start(mur[:], mur_d.ap()), "dms", 16)
        ld_ms = em.op("sync", lambda E: E.dma_start(muk[:], muk_d.ap()), "dms", 16)
        em.waits("vector", [ld_ms])
        em.op("vector", lambda E: E.tensor_scalar(murc[:], mur[:], -1.0, 1.0,
                                                  ALU.mult, ALU.add))
        ev_muc = em.op("vector", lambda E: E.tensor_scalar(mukc[:], muk[:], -1.0, 1.0,
                                                           ALU.mult, ALU.add))
        em.flush()

        BK.excluded = {6}
        PKV = pp[6]
        em.waits("tensor", [ev_ones, ev_b])
        em.op("tensor", lambda E: E.matmul(
            PKV[:, 0:512], zrow[0:1, 0:128], bqkv[0:1, 0:512],
            start=True, stop=False, skip_group_check=True))
        ks_cur = 0
        ev_xbf_free = {0: None, 1: None}
        ev_kv_last = None
        ev_ks_last = None
        st_a = {"xbfev": {}, "kqfree": {0: None, 1: None},
                "vqfree": {0: None, 1: None}, "kkern": {}, "vout": {},
                "tmpf_free": None, "ksl": None}
        xbufs = [xbf[0], xbf_a]

        def a_pref(i):
            if i >= nta:
                return
            em.wait("gpsimd", ev_xbf_free[i % 2])
            st_a["xbfev"][i % 2] = em.op(
                "gpsimd", lambda E, i=i, b=xbufs[i % 2]: E.dma_start(
                    b[:], x_tile_ap(xT, i)), "dxb%d" % (i % 2), 16)

        def a_proj(gs):
            i, s = divmod(gs, 2)
            sl = gs % 2
            xb = xbufs[i % 2]
            ev_xbf = st_a["xbfev"][i % 2]
            em.waits("tensor", [ev_xbf, ev_b, ev_ones])
            kb = [BK.acq(), BK.acq()]
            for hh in range(2):
                em.op("tensor", lambda E, b=kb[hh], hh=hh: E.matmul(
                    pp[b][:], ones_rb[0:1, 0:128],
                    bqkv[0:1, C + 512 * hh: C + 512 * hh + 512],
                    start=True, stop=False, skip_group_check=True))
            for j in range(NCH):
                em.wait("tensor", ev_wk[j // 2])
                for hh in range(2):
                    em.op("tensor", lambda E, b=kb[hh], hh=hh, j=j, s=s, xb=xb: E.matmul(
                        pp[b][:],
                        xb[:, 256 * j + 128 * s: 256 * j + 128 * s + 128],
                        w_sb["wk"][:, C * j + 512 * hh: C * j + 512 * hh + 512],
                        start=False, stop=(j == NCH - 1), skip_group_check=True))
            ev_kp = em.close("tensor", "pe")

            em.wait("vector", ev_kp)
            for hh in range(2):
                em.op("vector", lambda E, b=kb[hh], hh=hh: E.reduce_max(
                    sm16[:, 8 * hh: 8 * hh + 8],
                    pp[b][:].rearrange("p (h d) -> p h d", h=8), axis=AX.X))
            ev_kmax = em.close("vector", "dve")

            em.waits("scalar", [ev_kp, st_a["tmpf_free"], ev_kmax])
            for hh in range(2):
                em.op("scalar", lambda E, b=kb[hh], hh=hh: E.activation(
                    tmpf[:, 512 * hh: 512 * hh + 512], pp[b][:], AF.Exp))
            ev_expk = em.close("scalar", "act")
            em.wait("scalar", ev_kmax)
            ev_em = em.op("scalar", lambda E: E.activation(
                sm16[:, 16:32], sm16[:, 0:16], AF.Exp, scale=-1.0))
            BK.rel(kb[0], [ev_kmax, ev_expk])
            BK.rel(kb[1], [ev_kmax, ev_expk])

            em.waits("vector", [ev_expk, ev_em, st_a["kqfree"][sl]])
            emB = bass.AP(sm16, 16, [[80, 128], [1, H], [0, HD]])
            ev_kkern = em.op("vector", lambda E, emB=emB, sl=sl: E.tensor_tensor(
                kq[:, 1024 * sl: 1024 * sl + 1024].rearrange("p (h d) -> p h d", h=H),
                tmpf[:].rearrange("p (h d) -> p h d", h=H),
                emB, ALU.mult))
            st_a["kkern"][gs] = ev_kkern
            st_a["tmpf_free"] = ev_kkern

            vb = [BK.acq(), BK.acq()]
            for hh in range(2):
                em.op("tensor", lambda E, b=vb[hh], hh=hh: E.matmul(
                    pp[b][:], ones_rb[0:1, 0:128],
                    bqkv[0:1, 2 * C + 512 * hh: 2 * C + 512 * hh + 512],
                    start=True, stop=False, skip_group_check=True))
            for j in range(NCH):
                em.wait("tensor", ev_wv[j // 4])
                for hh in range(2):
                    em.op("tensor", lambda E, b=vb[hh], hh=hh, j=j, s=s, xb=xb: E.matmul(
                        pp[b][:],
                        xb[:, 256 * j + 128 * s: 256 * j + 128 * s + 128],
                        w_sb["wv"][:, C * j + 512 * hh: C * j + 512 * hh + 512],
                        start=False, stop=(j == NCH - 1), skip_group_check=True))
            ev_vp = em.close("tensor", "pe")
            if s == NS - 1:
                ev_xbf_free[i % 2] = ev_vp
                a_pref(i + 2)
            em.waits("vector", [ev_vp, st_a["vqfree"][sl]])
            for hh in range(2):
                em.op("vector", lambda E, b=vb[hh], hh=hh, sl=sl: E.tensor_copy(
                    qsb[:, 1024 * sl + 512 * hh: 1024 * sl + 512 * hh + 512],
                    pp[b][:]))
            ev_vout = em.close("vector", "dve")
            st_a["vout"][gs] = ev_vout
            BK.rel(vb[0], [ev_vout])
            BK.rel(vb[1], [ev_vout])

        def a_kv(gs):
            nonlocal ks_cur, ev_kv_last, ev_ks_last
            sl = gs % 2
            em.waits("tensor", [st_a["kkern"].pop(gs), st_a["vout"].pop(gs)])
            for h in range(H):
                r, jj = h % 2, h // 2
                em.op("tensor", lambda E, r=r, jj=jj, h=h, sl=sl: E.matmul(
                    PKV[64 * r: 64 * r + 64, 64 * jj: 64 * jj + 64],
                    kq[:, 1024 * sl + HD * h: 1024 * sl + HD * h + HD],
                    qsb[:, 1024 * sl + HD * h: 1024 * sl + HD * h + HD],
                    start=False, stop=False, skip_group_check=True))
            ev_kv = em.close("tensor", "pe")
            ev_kv_last = ev_kv
            st_a["vqfree"][sl] = ev_kv

            sb_ = [BK.acq(), BK.acq()]
            for hh in range(2):
                em.op("tensor", lambda E, b=sb_[hh], hh=hh, sl=sl: E.matmul(
                    pp[b][0:1, 0:512], ones_cb[:, 0:1],
                    kq[:, 1024 * sl + 512 * hh: 1024 * sl + 512 * hh + 512],
                    start=True, stop=True, skip_group_check=True))
            ev_ksmm = em.close("tensor", "pe")
            st_a["kqfree"][sl] = ev_ksmm
            nxt = 1 - ks_cur
            em.waits("vector", [ev_ksmm, ev_init, ev_ks_last])
            for hh in range(2):
                em.op("vector", lambda E, hh=hh, cur=ks_cur, nxt=nxt, b=sb_[hh]:
                      E.tensor_tensor(
                          ksacc[0:1, C * nxt + 512 * hh: C * nxt + 512 * hh + 512],
                          ksacc[0:1, C * cur + 512 * hh: C * cur + 512 * hh + 512],
                          pp[b][0:1, 0:512], ALU.add))
            ev_ks = em.close("vector", "dve")
            ev_ks_last = ev_ks
            ks_cur = nxt
            BK.rel(sb_[0], [ev_ks])
            BK.rel(sb_[1], [ev_ks])

        a_pref(0)
        a_pref(1)
        for gs in range(2 * nta):
            a_proj(gs)
            if gs >= 1:
                a_kv(gs - 1)
            if gs >= 9 and (gs - 9) % 10 == 0 and (gs - 9) // 10 < 5:
                load_B_weight(bw_names[(gs - 9) // 10])
            if nta < 5 and gs == 2 * nta - 1:
                for n in bw_names[max(0, (gs - 9) // 10 + 1) if gs >= 9 else 0:]:
                    load_B_weight(n)
            if gs % 2 == 1:
                em.flush()
        if nta:
            a_kv(2 * nta - 1)
        em.flush()

        em.wait("vector", ev_kv_last)
        ev_kvsb = em.op("vector", lambda E: E.tensor_copy(kv_sb[:], PKV[:, 0:512]))
        BK.excluded = set()
        BK.rel(6, [ev_kvsb])
        em.wait("vector", ev_ks_last)
        ev_kfb = em.op("vector", lambda E, cur=ks_cur: E.tensor_copy(
            kfb[0:1, :], ksacc[0:1, C * cur: C * cur + C]))
        em.wait("tensor", ev_kfb)
        bb_ = [BK.acq(), BK.acq()]
        for hh in range(2):
            em.op("tensor", lambda E, b=bb_[hh], hh=hh: E.matmul(
                pp[b][:], ones_rb[0:1, 0:128], kfb[0:1, 512 * hh: 512 * hh + 512],
                start=True, stop=True, skip_group_check=True))
        ev_ksb_mm = em.close("tensor", "pe")
        em.wait("vector", ev_ksb_mm)
        for hh in range(2):
            em.op("vector", lambda E, hh=hh, b=bb_[hh]: E.tensor_copy(
                ksumB[:, 512 * hh: 512 * hh + 512], pp[b][:]))
        ev_ksumB = em.close("vector", "dve")
        BK.rel(bb_[0], [ev_ksumB])
        BK.rel(bb_[1], [ev_ksumB])
        em.flush()

        xbfa_cm.__exit__(None, None, None)
        wv_cm.__exit__(None, None, None)
        wk_cm.__exit__(None, None, None)
        ksacc_cm.__exit__(None, None, None)
        kfb_cm.__exit__(None, None, None)
        pe_after_A = ("pe", em.cnt["pe"])

        xf32 = [e(nc.sbuf_tensor(f"xf32_{k}", [128, 2048], F32)) for k in range(2)]
        r1 = e(nc.sbuf_tensor("r1", [128, 2048], F32))
        x1 = [e(nc.sbuf_tensor(f"x1_{k}", [128, 2048], F32)) for k in range(2)]
        rsig = e(nc.sbuf_tensor("rsig", [128, 2048], F32))
        kact = e(nc.sbuf_tensor("kact", [128, 2048], BF16))
        xr = [e(nc.sbuf_tensor(f"xr{k}", [128, 2048], BF16)) for k in range(2)]
        xk = [e(nc.sbuf_tensor(f"xk{k}", [128, 2048], BF16)) for k in range(2)]
        rbf = e(nc.sbuf_tensor("rbf", [128, 2048], BF16))
        rbf2 = e(nc.sbuf_tensor("rbf2", [128, 2048], BF16))
        bcb = e(nc.sbuf_tensor("bcb", [128, 1024], F32))
        rlsc = e(nc.sbuf_tensor("rlsc", [128, 256], F32))
        vq2 = [e(nc.sbuf_tensor(f"vq2_{k}", [128, 2048], BF16)) for k in range(2)]
        ysb = e(nc.sbuf_tensor("ysb", [128, 2048], BF16))

        BK.small = (6, 7)
        BK.qpair = (4, 5)
        evB = {
            "xbf_free": dict(ev_xbf_free),
            "xf_load": {},
            "xbf_cast": {},
            "st1": {},
            "ln2": {},
            "bcb1": None,
            "rbf2_free": None,
            "qsb_free": pe_after_A,
            "kq_free": [pe_after_A, pe_after_A],
            "qT_free": {0: ev_kv_last, 1: ev_kv_last},
            "y_free": None,
            "rec_free": None,
            "tmpf_pool_guard": ("dve", em.cnt["dve"]),
            "x1_free": {0: [pe_after_A], 1: [pe_after_A]},
            "xr_free": {0: None, 1: None},
            "xk_free": {0: None, 1: None},
            "carry_free": {0: None, 1: None},
            "carry_ready": {},
            "stats_free": None,
            "rbf_free": None,
            "rsig_free": pe_after_A,
            "r1_free": pe_after_A,
            "bcb1_free": None,
            "bcb2_free": None,
            "x1_ready": {},
            "xr_ready": {},
            "xk_ready": {},
            "tp_done": {},
            "qsc_done": {},
            "qsc1_pend": {},
            "q_pend": {},
            "p2": {},
            "sd1_ev": None,
            "ln2c": {},
            "qsc1_pend": {},
            "p4b": {},
            "r2a_ev": {},
            "ch1": {},
            "ch2": {},
            "st2": {},
            "mean2_free": None,
            "qsc_pend": {},
            "r1_evts": {},
            "mexp_sm_free": None,
        }

        for t0 in range(min(2, ntb)):
            em.wait("sync", pe_after_A)
            evB["xf_load"][t0] = em.op(
                "sync", lambda E, t0=t0, sl=t0 % 2: E.dma_start(
                    xf32[sl][:], x_tile_ap(xT, t0)), "dxf%d" % (t0 % 2), 16)
        if ntb:
            em.waits("scalar", [evB["xbf_free"][0], evB["xf_load"][0]])
            evB["xbf_cast"][0] = em.op(
                "scalar", lambda E: E.copy(xbf[0][:], xf32[0][:]))

        def part1a_s(i, s):
            sl = i % 2
            if True:
                ev_xbf = evB["xbf_cast"][i] if s == 0 else evB["xbf_cast"].pop(i)
                em.waits("tensor", [ev_xbf, ev_w["wq"], ev_b, ev_ones])
                qb = [BK.acq_q(), BK.acq_q()]
                for hh in range(2):
                    em.op("tensor", lambda E, b=qb[hh], hh=hh: E.matmul(
                        pp[b][:], ones_rb[0:1, 0:128],
                        bqkv[0:1, 512 * hh: 512 * hh + 512],
                        start=True, stop=False, skip_group_check=True))
                for j in range(NCH):
                    for hh in range(2):
                        em.op("tensor", lambda E, b=qb[hh], hh=hh, j=j, s=s, sl=sl: E.matmul(
                            pp[b][:],
                            xbf[0][:, 256 * j + 128 * s: 256 * j + 128 * s + 128],
                            w_sb["wq"][:, C * j + 512 * hh: C * j + 512 * hh + 512],
                            start=False, stop=(j == NCH - 1), skip_group_check=True))
                ev_qp = em.close("tensor", "pe")

                em.waits("scalar", [ev_qp] + [evB["kq_free"][s]])
                for hh in range(2):
                    em.op("scalar", lambda E, hh=hh, s=s, b=qb[hh]: E.activation(
                        kq[:, 1024 * s + 512 * hh: 1024 * s + 512 * hh + 512],
                        pp[b][:], AF.Exp))
                ev_expq = em.close("scalar", "act")
                BK.rel(qb[0], [ev_expq])
                BK.rel(qb[1], [ev_expq])
                if s == NS - 1:
                    evB["xbf_free"][0] = ev_qp
                evB["q_pend"].setdefault(i, []).append((s, ev_expq))

        def part1b_chain(i):
            for s, ev_expq in evB["q_pend"].pop(i):
                kqs = kq[:, 1024 * s: 1024 * s + 1024]
                em.waits("vector", [ev_expq, ev_ksumB])
                em.op("vector", lambda E, kqs=kqs: E.reduce_max(
                    sm16[:, 0:16], kqs.rearrange("p (h d) -> p h d", h=H),
                    axis=AX.X))
                em.op("vector", lambda E, kqs=kqs: E.tensor_tensor(
                    tmpf[:], kqs, ksumB[:], ALU.mult))
                ev_dred = em.op("vector", lambda E: E.reduce_sum(
                    sm16[:, 16:32], tmpf[:].rearrange("p (h d) -> p h d", h=H),
                    axis=AX.X))
                em.op("vector", lambda E: E.scalar_tensor_tensor(
                    sm16[:, 32:48], sm16[:, 0:16], AT_EPS, sm16[:, 16:32],
                    ALU.mult, ALU.add))
                ev_rec = em.op("vector", lambda E, s=s: E.reciprocal(
                    sm16[:, 48 + 16 * s: 64 + 16 * s], sm16[:, 32:48]))
                if s == 0:
                    em.waits("vector", [evB["qsb_free"]])
                    recB = bass.AP(sm16, 48, [[80, 128], [1, H], [0, HD]])
                    ev_qsc = em.op("vector", lambda E, s=s, kqs=kqs, recB=recB:
                                   E.tensor_tensor(
                        qsb[:, 1024 * s: 1024 * s + 1024].rearrange(
                            "p (h d) -> p h d", h=H),
                        kqs.rearrange("p (h d) -> p h d", h=H),
                        recB, ALU.mult))
                    evB["kq_free"][s] = ev_qsc
                else:
                    evB["qsc1_pend"][i] = kqs

        def emit_qsc1():
            if not evB["qsc1_pend"]:
                return
            i, kqs = evB["qsc1_pend"].popitem()
            s = NS - 1
            em.waits("vector", [evB["qsb_free"]])
            recB = bass.AP(sm16, 48 + 16 * s, [[80, 128], [1, H], [0, HD]])
            ev_qsc = em.op("vector", lambda E, s=s, kqs=kqs, recB=recB:
                           E.tensor_tensor(
                qsb[:, 1024 * s: 1024 * s + 1024].rearrange(
                    "p (h d) -> p h d", h=H),
                kqs.rearrange("p (h d) -> p h d", h=H),
                recB, ALU.mult))
            evB["kq_free"][s] = ev_qsc
            evB["qsc_done"][i] = ev_qsc

        def part2_uwo(i):
            sl = i % 2
            ev_xf = evB["xf_load"][i]
            if i + 1 < ntb:
                em.waits("scalar", [evB["xbf_free"][0], evB["xf_load"][i + 1]])
                evB["xbf_cast"][i + 1] = em.op(
                    "scalar", lambda E, sl2=(i + 1) % 2: E.copy(xbf[0][:], xf32[sl2][:]))
            ev_yevac = []
            tpd = evB["tp_done"].pop(i)
            em.waits("tensor", [ev_kvsb, evB["y_free"]])
            for bk in range(4):
                em.waits("tensor", tpd[bk // 2])
                b = BK.acq()
                for pj in (2 * bk, 2 * bk + 1):
                    for r in range(2):
                        h = 2 * pj + r
                        em.op("tensor", lambda E, b=b, pj=pj, r=r, h=h, sl=sl: E.matmul(
                            pp[b][64 * r: 64 * r + 64,
                                  256 * (pj % 2): 256 * (pj % 2) + 256],
                            kv_sb[64 * r: 64 * r + 64,
                                  64 * (h // 2): 64 * (h // 2) + 64],
                            vq2[sl][64 * r: 64 * r + 64,
                                    256 * (h // 2): 256 * (h // 2) + 256],
                            start=True, stop=False, skip_group_check=True))
                ev_u = em.close("tensor", "pe")
                em.waits("vector", [ev_u])
                ev_ye = em.op("vector", lambda E, b=b, bk=bk: E.tensor_copy(
                    ysb[:, 512 * bk: 512 * bk + 512], pp[b][:]))
                ev_yevac.append(ev_ye)
                BK.rel(b, [ev_ye])
            evB["qT_free"][sl] = ("pe", em.cnt["pe"])
            evB["p2"][i] = (ev_yevac, ev_xf, sl)

        def part2_wo(i):
            ev_yevac, ev_xf, sl = evB["p2"].pop(i)
            em.waits("tensor", [ev_w["wo"]])
            ev_r1 = {}
            for ec in range(NCH):
                wob = BK.acq()
                for cj in range(NCH):
                    em.wait("tensor", ev_yevac[cj // 2])
                    em.op("tensor", lambda E, b=wob, ec=ec, cj=cj: E.matmul(
                        pp[b][:, 0:256],
                        w_sb["wo"][:, C * cj + 128 * ec: C * cj + 128 * ec + 128],
                        ysb[:, 256 * cj: 256 * cj + 256],
                        start=(cj == 0), stop=(cj == NCH - 1), skip_group_check=True))
                ev_wo = em.close("tensor", "pe")
                em.waits("vector", [ev_wo, ev_xf, ld_ms, evB["r1_free"]])
                ev_r1[ec] = em.op("vector", lambda E, b=wob, ec=ec, sl=sl:
                                  E.scalar_tensor_tensor(
                    r1[:, 256 * ec: 256 * ec + 256],
                    pp[b][:, 0:256],
                    bo8[:, ec: ec + 1],
                    xf32[sl][:, 256 * ec: 256 * ec + 256],
                    ALU.add, ALU.add))
                BK.rel(wob, [ev_r1[ec]])
            evB["y_free"] = ("pe", em.cnt["pe"])
            evB["r1_evts"][i] = ev_r1
            em.wait("scalar", evB["rbf_free"])
            ev_r1bf = {}
            for ec in range(NCH):
                em.wait("scalar", ev_r1[ec])
                ev_r1bf[ec] = em.op("scalar", lambda E, ec=ec: E.copy(
                    rbf[:, 256 * ec: 256 * ec + 256],
                    r1[:, 256 * ec: 256 * ec + 256]))
            stb = BK.acq_small()
            for ec in range(NCH):
                em.wait("tensor", ev_r1bf[ec])
                em.op("tensor", lambda E, b=stb, ec=ec: E.matmul(
                    pp[b][0:1, 0:256], ones_cb[:, 0:1],
                    rbf[:, 256 * ec: 256 * ec + 256],
                    start=(ec == 0), stop=(ec == NCH - 1), skip_group_check=True))
            ev_sum1 = em.close("tensor", "pe")
            em.wait("scalar", ev_sum1)
            ev_sq1a = em.op("scalar", lambda E: E.square(
                rbf[:, 0:1024], r1[:, 0:1024]))
            ev_sq1b = em.op("scalar", lambda E: E.square(
                rbf[:, 1024:2048], r1[:, 1024:2048]))
            evB["st1"][i] = (stb, ev_sq1a, ev_sq1b)
            evB["xf_load"].pop(i)
            if i + 2 < ntb:
                em.wait("sync", ev_r1[NCH - 1])
                evB["xf_load"][i + 2] = em.op(
                    "sync", lambda E, i=i, sl=sl: E.dma_start(
                        xf32[sl][:], x_tile_ap(xT, i + 2)), "dxf%d" % sl, 16)

        def part4a_chain1(i):
            stb, ev_sq1a, ev_sq1b = evB["st1"].pop(i)
            for ec in range(NCH):
                em.wait("tensor", ev_sq1a if ec < 4 else ev_sq1b)
                em.op("tensor", lambda E, b=stb, ec=ec: E.matmul(
                    pp[b][0:1, 256:512], ones_cb[:, 0:1],
                    rbf[:, 256 * ec: 256 * ec + 256],
                    start=(ec == 0), stop=(ec == NCH - 1), skip_group_check=True))
            ev_stats1 = em.close("tensor", "pe")
            evB["rbf_free"] = ev_stats1

            em.waits("vector", [ev_stats1, evB["stats_free"]])
            em.op("vector", lambda E, b=stb: E.tensor_scalar(
                stats[0:1, 0:256], pp[b][0:1, 0:256], 1.0 / C, None, ALU.mult))
            em.op("vector", lambda E, b=stb: E.tensor_scalar(
                stats[0:1, 256:512], pp[b][0:1, 256:512], 1.0 / C, None, ALU.mult))
            ev_ms = em.close("vector", "dve")
            BK.rel(stb, [ev_ms])
            em.op("vector", lambda E: E.tensor_tensor(
                stats[0:1, 512:768], stats[0:1, 0:256], stats[0:1, 0:256], ALU.mult))
            ev_var = em.op("vector", lambda E: E.tensor_tensor(
                stats[0:1, 768:1024], stats[0:1, 256:512], stats[0:1, 512:768],
                ALU.subtract))
            em.wait("scalar", ev_var)
            em.op("scalar", lambda E: E.activation(
                stats[0:1, 512:768], stats[0:1, 768:1024], AF.Ln,
                bias=cst[0:1, 0:1]))
            ev_sd = em.op("scalar", lambda E: E.activation(
                stats[0:1, 256:512], stats[0:1, 512:768], AF.Exp, scale=-0.5))
            evB["sd1_ev"] = ev_sd
            evB["ch1"][i] = (ev_sd, ev_ms)

        def part4a_bc1(i):
            ev_sd, ev_ms = evB["ch1"].pop(i)
            em.waits("tensor", [ev_sd, ev_ms])
            bcbk = BK.acq_small()
            em.op("tensor", lambda E, b=bcbk: E.matmul(
                pp[b][:, 0:256], ones_rf[0:1, :], stats[0:1, 256:512],
                start=True, stop=False, skip_group_check=True))
            em.op("tensor", lambda E, b=bcbk: E.matmul(
                pp[b][:, 256:512], nones_rf[0:1, :], stats[0:1, 0:256],
                start=True, stop=True, skip_group_check=True))
            ev_bc1 = em.close("tensor", "pe")
            evB["stats_free"] = ev_bc1
            em.waits("vector", [ev_bc1, evB["bcb1_free"]])
            ev_bcb1 = em.op("vector", lambda E, b=bcbk: E.tensor_copy(
                bcb[:, 0:512], pp[b][:]))
            BK.rel(bcbk, [ev_bcb1])
            evB["bcb1"] = ev_bcb1

        def part4b_norm1mix(i):
            sl = i % 2
            ev_r1 = evB["r1_evts"].pop(i)
            ev_bcb1 = evB["bcb1"]
            em.waits("gpsimd", [ev_bcb1, ev_r1[NCH - 1]])
            em.waits("gpsimd", evB["x1_free"][sl])
            rstdB = bass.AP(bcb, 0, [[1024, 128], [0, NCH], [1, 256]])
            negmB = bass.AP(bcb, 256, [[1024, 128], [0, NCH], [1, 256]])
            em.op("gpsimd", lambda E, sl=sl, negmB=negmB: E.tensor_tensor(
                x1[sl][:].rearrange("p (j t) -> p j t", j=NCH),
                r1[:].rearrange("p (j t) -> p j t", j=NCH), negmB, ALU.add))
            ev_x1 = em.op("gpsimd", lambda E, sl=sl, rstdB=rstdB: E.tensor_tensor(
                x1[sl][:].rearrange("p (j t) -> p j t", j=NCH),
                x1[sl][:].rearrange("p (j t) -> p j t", j=NCH), rstdB, ALU.mult))
            evB["x1_ready"][i] = ev_x1
            evB["r1_free"] = ev_x1
            evB["bcb1_free"] = ev_x1

            wh = (i + 1) % 2
            em.waits("vector", [ev_x1] + (evB["carry_free"][wh] or []))
            for j in range(NCH):
                em.op("vector", lambda E, j=j, wh=wh, sl=sl: E.tensor_copy(
                    carry[:, 8 * wh + j: 8 * wh + j + 1],
                    x1[sl][:, 256 * j + 255: 256 * j + 256]))
            ev_carry = em.close("vector", "dve")
            evB["carry_ready"][i + 1] = ev_carry

            rh = i % 2
            ev_cp = evB["carry_ready"].pop(i, ev_muc)
            em.waits("gpsimd", [ev_x1, ev_cp, ld_ms, evB["xr_free"][sl],
                                evB["xk_free"][sl]])
            crv = carry[:, 8 * rh: 8 * rh + 8].rearrange("p (j o) -> p j o", o=1)
            x1v = x1[sl][:].rearrange("p (j t) -> p j t", j=NCH)
            em.op("gpsimd", lambda E, sl=sl: E.tensor_tensor(
                rbf[:, 1:2048], x1[sl][:, 1:2048], x1[sl][:, 0:2047], ALU.subtract))
            em.op("gpsimd", lambda E, x1v=x1v, crv=crv: E.tensor_tensor(
                rbf[:].rearrange("p (j t) -> p j t", j=NCH)[:, :, 0:1],
                x1v[:, :, 0:1], crv, ALU.subtract))
            for dst, mm in ((xr, mur), (xk, muk)):
                muB = bass.AP(mm, 0, [[8, 128], [1, NCH], [0, 256]])
                em.op("gpsimd", lambda E, dst=dst, sl=sl, muB=muB: E.tensor_tensor(
                    dst[sl][:].rearrange("p (j t) -> p j t", j=NCH),
                    rbf[:].rearrange("p (j t) -> p j t", j=NCH), muB, ALU.mult))
                em.op("gpsimd", lambda E, dst=dst, sl=sl, x1v=x1v: E.tensor_tensor(
                    dst[sl][:].rearrange("p (j t) -> p j t", j=NCH)[:, :, 1:256],
                    dst[sl][:].rearrange("p (j t) -> p j t", j=NCH)[:, :, 1:256],
                    x1v[:, :, 0:255], ALU.add))
                em.op("gpsimd", lambda E, dst=dst, sl=sl, crv=crv: E.tensor_tensor(
                    dst[sl][:].rearrange("p (j t) -> p j t", j=NCH)[:, :, 0:1],
                    dst[sl][:].rearrange("p (j t) -> p j t", j=NCH)[:, :, 0:1],
                    crv, ALU.add))
            ev_mix = em.close("gpsimd", "pool")
            evB["rbf_free"] = [evB["rbf_free"], ev_mix] \
                if evB["rbf_free"] else [ev_mix]
            evB["xr_ready"][i] = ev_mix
            evB["xk_ready"][i] = ev_mix
            evB["carry_free"][rh] = [ev_mix]

        def part3_ffn(i):
            sl = i % 2
            em.waits("tensor", [evB["xk_ready"].pop(i), ev_w["wk2"]])
            ev_kact = None
            for dc in range(NCH):
                kb_ = BK.acq()
                for cj in range(NCH):
                    em.op("tensor", lambda E, b=kb_, dc=dc, cj=cj, sl=sl: E.matmul(
                        pp[b][:, 0:256],
                        w_sb["wk2"][:, C * cj + 128 * dc: C * cj + 128 * dc + 128],
                        xk[sl][:, 256 * cj: 256 * cj + 256],
                        start=(cj == 0), stop=(cj == NCH - 1), skip_group_check=True))
                ev_kp = em.close("tensor", "pe")
                em.wait("scalar", ev_kp)
                ev_rl = em.op("scalar", lambda E, b=kb_, dc=dc: E.activation(
                    rlsc[:, 0:256], pp[b][:, 0:256], AF.Relu))
                ev_kact = em.op("scalar", lambda E, dc=dc: E.square(
                    kact[:, 256 * dc: 256 * dc + 256], rlsc[:, 0:256]))
                BK.rel(kb_, [ev_rl])
            evB["xk_free"][sl] = ("pe", em.cnt["pe"])

            em.waits("tensor", [evB["xr_ready"].pop(i), ev_w["wr"]])
            ev_sig = {}
            for dc in range(NCH):
                rb_ = BK.acq()
                for cj in range(NCH):
                    em.op("tensor", lambda E, b=rb_, dc=dc, cj=cj, sl=sl: E.matmul(
                        pp[b][:, 0:256],
                        w_sb["wr"][:, C * cj + 128 * dc: C * cj + 128 * dc + 128],
                        xr[sl][:, 256 * cj: 256 * cj + 256],
                        start=(cj == 0), stop=(cj == NCH - 1), skip_group_check=True))
                ev_rp = em.close("tensor", "pe")
                em.waits("scalar", [ev_rp, evB["rsig_free"]])
                ev_sig[dc] = em.op("scalar", lambda E, b=rb_, dc=dc: E.activation(
                    rsig[:, 256 * dc: 256 * dc + 256],
                    pp[b][:, 0:256], AF.Sigmoid))
                BK.rel(rb_, [ev_sig[dc]])
            evB["xr_free"][sl] = ("pe", em.cnt["pe"])

            em.waits("tensor", [ev_kact, ev_w["wv2"]])
            ev_r2 = {}
            for ec in range(NCH):
                vb_ = BK.acq_ffn()
                for cj in range(NCH):
                    em.op("tensor", lambda E, b=vb_, ec=ec, cj=cj: E.matmul(
                        pp[b][:, 0:256],
                        w_sb["wv2"][:, C * cj + 128 * ec: C * cj + 128 * ec + 128],
                        kact[:, 256 * cj: 256 * cj + 256],
                        start=(cj == 0), stop=(cj == NCH - 1), skip_group_check=True))
                ev_vp = em.close("tensor", "pe")
                em.waits("vector", [ev_vp, ev_sig[ec]])
                ev_r2[ec] = em.op("vector", lambda E, b=vb_, ec=ec: E.tensor_tensor(
                    rsig[:, 256 * ec: 256 * ec + 256],
                    rsig[:, 256 * ec: 256 * ec + 256],
                    pp[b][:, 0:256], ALU.mult))
                BK.rel(vb_, [ev_r2[ec]])
                if ec == 1:
                    emit_qsc1()
            em.waits("vector", [evB["x1_ready"][i]])
            ev_r2a_a = em.op("vector", lambda E, sl=sl: E.tensor_tensor(
                rsig[:, 0:1024], rsig[:, 0:1024], x1[sl][:, 0:1024], ALU.add))
            em.waits("scalar", [ev_r2a_a, evB["rbf2_free"]])
            ev_r2bf_a = em.op("scalar", lambda E: E.copy(
                rbf2[:, 0:1024], rsig[:, 0:1024]))
            ev_r2a = em.op("vector", lambda E, sl=sl: E.tensor_tensor(
                rsig[:, 1024:2048], rsig[:, 1024:2048], x1[sl][:, 1024:2048],
                ALU.add))
            em.wait("scalar", ev_r2a)
            ev_r2bf = em.op("scalar", lambda E: E.copy(
                rbf2[:, 1024:2048], rsig[:, 1024:2048]))
            evB["x1_free"][sl] = [ev_r2a]
            evB["x1_ready"].pop(i)
            evB["r2a_ev"][i] = ev_r2a
            evB["ln2"][i] = (ev_r2bf_a, ev_r2bf, ev_r2a)

        def part3b_pre(i):
            ev_r2bf_a, ev_r2bf, ev_r2a = evB["ln2"].pop(i)
            stb = BK.acq_small()
            for ec in range(NCH):
                em.wait("tensor", ev_r2bf_a if ec < 4 else ev_r2bf)
                em.op("tensor", lambda E, b=stb, ec=ec: E.matmul(
                    pp[b][0:1, 0:256], ones_cb[:, 0:1],
                    rbf2[:, 256 * ec: 256 * ec + 256],
                    start=(ec == 0), stop=(ec == NCH - 1), skip_group_check=True))
            ev_sum2 = em.close("tensor", "pe")
            em.wait("scalar", ev_sum2)
            ev_sq2 = em.op("scalar", lambda E: E.square(rbf2[:], rsig[:]))
            em.waits("vector", [ev_sum2, evB["mean2_free"]])
            ev_m2 = em.op("vector", lambda E, b=stb: E.tensor_scalar(
                stats[0:1, 1024:1280], pp[b][0:1, 0:256], 1.0 / C, None, ALU.mult))
            BK.rel(stb, [ev_m2])
            evB["st2"][i] = (ev_m2, ev_sq2, ev_r2a)

        def part3b_ln2mm(i):
            ev_m2, ev_sq2, ev_r2a = evB["st2"].pop(i)
            stb = BK.acq_small()
            em.wait("tensor", ev_sq2)
            for ec in range(NCH):
                em.op("tensor", lambda E, b=stb, ec=ec: E.matmul(
                    pp[b][0:1, 256:512], ones_cb[:, 0:1],
                    rbf2[:, 256 * ec: 256 * ec + 256],
                    start=(ec == 0), stop=(ec == NCH - 1), skip_group_check=True))
            ev_stats2 = em.close("tensor", "pe")
            evB["rbf2_free"] = ev_stats2

            em.waits("vector", [ev_stats2, ev_m2])
            em.op("vector", lambda E, b=stb: E.tensor_scalar(
                stats[0:1, 1280:1536], pp[b][0:1, 256:512], 1.0 / C, None, ALU.mult))
            ev_ms = em.close("vector", "dve")
            BK.rel(stb, [ev_ms])
            evB["ln2c"][i] = (ev_ms, ev_m2, ev_r2a, ev_sq2)

        def part3b_ln2(i):
            ev_ms, ev_m2, ev_r2a, ev_sq2 = evB["ln2c"].pop(i)
            em.waits("vector", [ev_ms, ev_m2])
            em.op("vector", lambda E: E.tensor_tensor(
                stats[0:1, 1536:1792], stats[0:1, 1024:1280], stats[0:1, 1024:1280],
                ALU.mult))
            em.wait("vector", evB["sd1_ev"])
            ev_var = em.op("vector", lambda E: E.tensor_tensor(
                stats[0:1, 512:768], stats[0:1, 1280:1536], stats[0:1, 1536:1792],
                ALU.subtract))
            em.wait("scalar", ev_var)
            em.op("scalar", lambda E: E.activation(
                stats[0:1, 1536:1792], stats[0:1, 512:768], AF.Ln,
                bias=cst[0:1, 0:1]))
            ev_sd = em.op("scalar", lambda E: E.activation(
                stats[0:1, 512:768], stats[0:1, 1536:1792], AF.Exp, scale=-0.5))
            evB["ch2"][i] = (ev_sd, ev_ms, ev_r2a, ev_sq2)

        def part3b_bc2(i):
            ev_sd, ev_ms, ev_r2a, ev_sq2 = evB["ch2"].pop(i)
            em.waits("tensor", [ev_sd, ev_ms])
            bcbk = BK.acq_small()
            em.op("tensor", lambda E, b=bcbk: E.matmul(
                pp[b][:, 0:256], ones_rf[0:1, :], stats[0:1, 512:768],
                start=True, stop=False, skip_group_check=True))
            em.op("tensor", lambda E, b=bcbk: E.matmul(
                pp[b][:, 256:512], nones_rf[0:1, :], stats[0:1, 1024:1280],
                start=True, stop=True, skip_group_check=True))
            ev_bc2 = em.close("tensor", "pe")
            evB["stats_free"] = ev_bc2
            evB["mean2_free"] = ev_bc2
            em.waits("vector", [ev_bc2, evB["bcb2_free"]])
            ev_bcb2 = em.op("vector", lambda E, b=bcbk: E.tensor_copy(
                bcb[:, 512:1024], pp[b][:]))
            BK.rel(bcbk, [ev_bcb2])

            em.waits("gpsimd", [ev_bcb2, ev_r2a, ev_sq2])
            rstdB2 = bass.AP(bcb, 512, [[1024, 128], [0, NCH], [1, 256]])
            negmB2 = bass.AP(bcb, 768, [[1024, 128], [0, NCH], [1, 256]])
            em.op("gpsimd", lambda E, negmB2=negmB2: E.tensor_tensor(
                rsig[:].rearrange("p (j t) -> p j t", j=NCH),
                rsig[:].rearrange("p (j t) -> p j t", j=NCH), negmB2, ALU.add))
            ev_out = em.op("gpsimd", lambda E, rstdB2=rstdB2: E.tensor_tensor(
                rsig[:].rearrange("p (j t) -> p j t", j=NCH),
                rsig[:].rearrange("p (j t) -> p j t", j=NCH), rstdB2, ALU.mult))
            evB["r2_free"] = ev_out
            evB["bcb2_free"] = ev_out

            em.wait("sync", ev_out)
            half = bass.AP(outT, T * i, [[S, 128], [128 * S, 4], [1, T]])
            half2 = bass.AP(outT, T * i + 4 * 128 * S, [[S, 128], [128 * S, 4], [1, T]])
            ev_s1 = em.op("sync", lambda E, half=half: E.dma_start(
                half, rsig[:, 0:1024]), "dst", 16)
            ev_s2 = em.op("sync", lambda E, half2=half2: E.dma_start(
                half2, rsig[:, 1024:2048]), "dst2", 16)
            evB["rsig_free"] = [ev_s1, ev_s2]

        def part5_tp(i):
            sl = i % 2
            emit_qsc1()
            em.waits("tensor", [evB["qsc_done"].pop(i), evB["qT_free"][sl], ev_id])
            evs = {0: [], 1: []}
            for g, (jg, s) in enumerate([(0, 0), (0, 1), (1, 0), (1, 1)]):
                b = BK.acq()
                for blk in range(4):
                    j = 4 * jg + blk
                    em.op("tensor", lambda E, b=b, s=s, j=j, blk=blk: E.matmul(
                        pp[b][:, 128 * blk: 128 * blk + 128],
                        qsb[:, 1024 * s + 128 * j: 1024 * s + 128 * j + 128],
                        id_sb[:],
                        start=True, stop=False, skip_group_check=True))
                ev_mm = em.close("tensor", "pe")
                dst = bass.AP(vq2[sl], 1024 * jg + 128 * s,
                              [[2048, 128], [256, 4], [1, 128]])
                if g % 2 == 0:
                    em.wait("vector", ev_mm)
                    ev_e = em.op("vector", lambda E, b=b, dst=dst: E.tensor_copy(
                        dst, pp[b][:]))
                else:
                    em.wait("scalar", ev_mm)
                    ev_e = em.op("scalar", lambda E, b=b, dst=dst: E.copy(
                        dst, pp[b][:]))
                evs[jg].append(ev_e)
                BK.rel(b, [ev_e])
            evB["qsb_free"] = ev_mm
            evB["tp_done"][i] = evs

        for it in range(ntb + 3):
            if 3 <= it <= ntb + 2:
                part3b_pre(it - 3)
            if 1 <= it <= ntb:
                part2_uwo(it - 1)
                part2_wo(it - 1)
            if 3 <= it <= ntb + 2:
                part3b_ln2mm(it - 3)
            if 1 <= it <= ntb:
                part4a_chain1(it - 1)
            if 3 <= it <= ntb + 2:
                part3b_ln2(it - 3)
            if it < ntb:
                part1a_s(it, 0)
            if 1 <= it <= ntb:
                part4a_bc1(it - 1)
            if 3 <= it <= ntb + 2:
                part3b_bc2(it - 3)
            if it < ntb:
                part1a_s(it, 1)
                part1b_chain(it)
            if 1 <= it <= ntb:
                part4b_norm1mix(it - 1)
            if 2 <= it <= ntb + 1:
                part3_ffn(it - 2)
            if it < ntb:
                part5_tp(it)
            em.flush()

        if dbg:
            kvdbg = nc.dram_tensor("kvdbg", [128, 512], F32, kind="ExternalOutput").ap()
            ksdbg = nc.dram_tensor("ksdbg", [128, C], F32, kind="ExternalOutput").ap()
            em.waits("gpsimd", [ev_kvsb, ev_ksumB])
            em.op("gpsimd", lambda E: E.dma_start(kvdbg, kv_sb[:]), "ddbg", 16)
            em.op("gpsimd", lambda E: E.dma_start(ksdbg, ksumB[:]), "ddbg", 16)
            em.raw("gpsimd", lambda E, v=em.cnt["ddbg"]: E.wait_ge(sems["ddbg"], v))
        v_final = em.cnt["dst"]
        em.raw("scalar", lambda E, v=v_final: E.wait_ge(sems["dst"], v))
        v_final2 = em.cnt["dst2"]
        em.raw("scalar", lambda E, v=v_final2: E.wait_ge(sems["dst2"], v))
        em.flush()

    build_program.last_em = em
    return nc


def kernel(**inputs):
    x = np.asarray(inputs["L0_attn_x"], dtype=np.float32)
    f32c = lambda a: np.ascontiguousarray(np.asarray(a, dtype=np.float32))
    wT = {
        "wq": f32c(np.asarray(inputs["L0_attn_Wq"]).T),
        "wk": f32c(np.asarray(inputs["L0_attn_Wk"]).T),
        "wv": f32c(np.asarray(inputs["L0_attn_Wv"]).T),
        "wo": f32c(np.asarray(inputs["L0_attn_Wo"]).T),
        "wr": f32c(np.asarray(inputs["L0_ffn_W_r"]).T),
        "wk2": f32c(np.asarray(inputs["L0_ffn_W_k"]).T),
        "wv2": f32c(np.asarray(inputs["L0_ffn_W_v"]).T),
    }
    bqkv = f32c(np.concatenate([np.asarray(inputs["L0_attn_bq"]),
                                np.asarray(inputs["L0_attn_bk"]),
                                np.asarray(inputs["L0_attn_bv"])]).reshape(1, 3 * C))
    bo8 = f32c(np.asarray(inputs["L0_attn_bo"]).reshape(8, 128).T)
    mur8 = f32c(np.asarray(inputs["L0_ffn_mu_r"]).reshape(8, 128).T)
    muk8 = f32c(np.asarray(inputs["L0_ffn_mu_k"]).reshape(8, 128).T)

    nc = build_program()
    in_maps = []
    for b in range(8):
        m = {"xT": f32c(x[b].T), "bqkv": bqkv, "bo8": bo8,
             "mur8": mur8, "muk8": muk8, "id128": np.eye(128, dtype=np.float32)}
        m.update(wT)
        in_maps.append(m)
    res = run_bass_kernel_spmd(nc, in_maps, core_ids=list(range(8)))
    out = np.stack([np.asarray(res.results[b]["outT"]).T for b in range(8)], axis=0)
    return np.ascontiguousarray(out, dtype=np.float32)


if __name__ == "__main__":
    if "--build" in sys.argv:
        import time
        t0 = time.time()
        nc = build_program()
        print(f"build ok in {time.time() - t0:.1f}s")

